# version 05
# baseline (speedup 1.0000x reference)
import numpy as np
import ml_dtypes
import concourse.bass as bass
import concourse.mybir as mybir
from concourse.bass_utils import run_bass_kernel_spmd

F32 = mybir.dt.float32
BF16 = mybir.dt.bfloat16
AF = mybir.ActivationFunctionType
ALU = mybir.AluOpType
AX = mybir.AxisListType
bfdt = ml_dtypes.bfloat16

PH0_PARTS = ("consts", "vaug", "bias", "slabs", "rw")

S = 2048
G = 512
NGRP = 4
D = 1024
C = 8
NE = 18
F = 512
FCH = 4
H = 16
DH = 64
NKT = 16
EPS = 1e-5
KB = 1024


class Buf:
    __slots__ = ("w", "r", "name")

    def __init__(self, name):
        self.name = name
        self.w = None
        self.r = {}


class Kb:
    ENGS = ("pe", "act", "dve", "pool", "sync")

    def __init__(self, nc):
        self.nc = nc
        self.e = {"pe": nc.tensor, "act": nc.scalar, "dve": nc.vector,
                  "pool": nc.gpsimd, "sync": nc.sync}
        names = (*self.ENGS, "dmaS", "dmaG", "wA", "wB", "wC",
                 "d0", "d1", "d2", "d3", "d4", "d5", "g0", "g1")
        self.sem = {n: nc.alloc_semaphore("s_" + n) for n in names}
        self.cnt = {n: 0 for n in names}
        self.hwm = {n: {m: 0 for m in names} for n in self.ENGS}
        self.bufs = []
        self.rot = {"sync": ["d0", "d1", "d2", "d3", "d4", "d5"],
                    "pool": ["g0", "g1"]}
        self.rix = {"sync": 0, "pool": 0}

    def buf(self, name):
        b = Buf(name)
        self.bufs.append(b)
        return b

    def _waits(self, eng, deps):
        best = {}
        for d in deps:
            if d is None:
                continue
            s, v = d
            if v and v > best.get(s, 0):
                best[s] = v
        for s, v in best.items():
            if self.hwm[eng][s] < v:
                self.e[eng].wait_ge(self.sem[s], v)
                self.hwm[eng][s] = v

    def op(self, eng, fn, reads=(), writes=(), extra=()):
        deps = list(extra)
        for b in reads:
            deps.append(b.w)
        for b in writes:
            deps.append(b.w)
            deps.extend(b.r.items())
        self._waits(eng, deps)
        ins = fn(self.e[eng])
        ins.then_inc(self.sem[eng], 1)
        self.cnt[eng] += 1
        v = self.cnt[eng]
        for b in reads:
            b.r[eng] = v
        for b in writes:
            b.w = (eng, v)
            b.r = {}
        return v

    def dma(self, q, out, in_, reads=(), writes=(), extra=(), **kw):
        eng = "sync" if q == "sync" else "pool"
        deps = list(extra)
        for b in reads:
            deps.append(b.w)
        for b in writes:
            deps.append(b.w)
            deps.extend(b.r.items())
        self._waits(eng, deps)
        sems = self.rot[eng]
        dsem = sems[self.rix[eng] % len(sems)]
        self.rix[eng] += 1
        if self.cnt[dsem] > 0:
            self._waits(eng, [(dsem, self.cnt[dsem])])
        self.e[eng].dma_start(out=out, in_=in_, **kw).then_inc(self.sem[dsem], 16)
        self.cnt[dsem] += 16
        v = self.cnt[dsem]
        for b in reads:
            b.r[dsem] = v
        for b in writes:
            b.w = (dsem, v)
            b.r = {}
        return v

    BAR_ENGS = ("pe", "act", "dve", "pool")

    def sync_join(self):
        for n in self.BAR_ENGS:
            if self.cnt[n] > 0:
                self.e["sync"].wait_ge(self.sem[n], self.cnt[n])
                self.hwm["sync"][n] = self.cnt[n]

    def barrier(self):
        import concourse.mybir as _mb
        self.nc.multi_engine_barrier([_mb.EngineType.PE, _mb.EngineType.Activation,
                                      _mb.EngineType.DVE, _mb.EngineType.Pool])
        for n in self.BAR_ENGS:
            for m in self.BAR_ENGS:
                self.hwm[n][m] = self.cnt[m]


def _rope_tables():
    inv = 1.0 / (10000.0 ** (np.arange(0, 32, 2, dtype=np.float64) / 32.0))
    t = np.arange(S, dtype=np.float64)
    fr = t[:, None] * inv[None, :]
    fr = np.repeat(fr, 2, axis=1)
    cos = np.ones((128, S), dtype=np.float64)
    sin = np.zeros((128, S), dtype=np.float64)
    for base in (0, 64):
        cos[base:base + 32, :] = np.cos(fr).T
        sin[base:base + 32, :] = np.sin(fr).T
    return cos.astype(bfdt), sin.astype(bfdt)


def _pswap():
    p = np.zeros((128, 128), dtype=np.float32)
    for base in (0, 64):
        for i in range(16):
            p[base + 2 * i + 1, base + 2 * i] = -1.0
            p[base + 2 * i, base + 2 * i + 1] = 1.0
    return p.astype(bfdt)


def build_nc(nseq=4, stage="full"):
    nc = bass.Bass("TRN2", target_bir_lowering=False, dynamic_dma_scratch_size=16384)

    Xd = nc.dram_tensor("L0_attn_X", [nseq, S, D], F32, kind="ExternalInput").ap()
    Wi_d = nc.dram_tensor("L0_attn_W_in", [D, 3 * D], F32, kind="ExternalInput").ap()
    bi_d = nc.dram_tensor("L0_attn_b_in", [3 * D], F32, kind="ExternalInput").ap()
    Wo_d = nc.dram_tensor("L0_attn_W_out", [D, D], F32, kind="ExternalInput").ap()
    bo_d = nc.dram_tensor("L0_attn_b_out", [D], F32, kind="ExternalInput").ap()
    ga_d = nc.dram_tensor("L0_attn_gamma", [D], F32, kind="ExternalInput").ap()
    be_d = nc.dram_tensor("L0_attn_beta", [D], F32, kind="ExternalInput").ap()
    rw_d = nc.dram_tensor("L0_ffn_router_w", [D, 16], F32, kind="ExternalInput").ap()
    gw_d = nc.dram_tensor("L0_ffn_gate_w", [16, D, F], F32, kind="ExternalInput").ap()
    uw_d = nc.dram_tensor("L0_ffn_up_w", [16, D, F], F32, kind="ExternalInput").ap()
    dw_d = nc.dram_tensor("L0_ffn_down_w", [16, F, D], F32, kind="ExternalInput").ap()
    sg_d = nc.dram_tensor("L0_ffn_shared_gate_w", [D, D], F32, kind="ExternalInput").ap()
    su_d = nc.dram_tensor("L0_ffn_shared_up_w", [D, D], F32, kind="ExternalInput").ap()
    sd_d = nc.dram_tensor("L0_ffn_shared_down_w", [D, D], F32, kind="ExternalInput").ap()
    OUT = nc.dram_tensor("OUT", [nseq, S, D], F32, kind="ExternalOutput").ap()

    wc_g = nc.dram_tensor("wc_g", [16, D, F], BF16, kind="Internal").ap()
    wc_u = nc.dram_tensor("wc_u", [16, D, F], BF16, kind="Internal").ap()
    wc_d = nc.dram_tensor("wc_d", [16, F, D], BF16, kind="Internal").ap()
    wc_sg = nc.dram_tensor("wc_sg", [D, D], BF16, kind="Internal").ap()
    wc_su = nc.dram_tensor("wc_su", [D, D], BF16, kind="Internal").ap()
    wc_sd = nc.dram_tensor("wc_sd", [D, D], BF16, kind="Internal").ap()
    wc_wi = nc.dram_tensor("wc_wi", [D, 3 * D], BF16, kind="Internal").ap()
    wc_wo = nc.dram_tensor("wc_wo", [D, D], BF16, kind="Internal").ap()
    wc_rw = nc.dram_tensor("wc_rw", [D, 16], BF16, kind="Internal").ap()

    cosT, sinT = _rope_tables()
    c_cos = nc.inline_tensor(cosT, name="c_cos").ap()
    c_sin = nc.inline_tensor(sinT, name="c_sin").ap()
    c_pswap = nc.inline_tensor(_pswap(), name="c_pswap").ap()
    c_idf = nc.inline_tensor(np.eye(128, dtype=np.float32), name="c_idf").ap()
    c_idb = nc.inline_tensor(np.eye(128, dtype=bfdt), name="c_idb").ap()
    c_ones_col = nc.inline_tensor(np.ones((128, 1), dtype=bfdt), name="c_ones_col").ap()
    c_ones_row = nc.inline_tensor(np.ones((1, 128), dtype=np.float32), name="c_ones_row").ap()
    c_ones64 = nc.inline_tensor(np.ones((1, 64), dtype=np.float32), name="c_ones64").ap()
    hd2 = np.zeros((128, 2), dtype=bfdt)
    hd2[0:64, 0] = 1.0
    hd2[64:128, 1] = 1.0
    c_hd2 = nc.inline_tensor(hd2, name="c_hd2").ap()
    sel = np.zeros((32, NE * 128), dtype=bfdt)
    for e in range(NE):
        sel[e, e * 128:(e + 1) * 128] = 1.0
    c_sel = nc.inline_tensor(sel, name="c_sel").ap()
    b2 = np.zeros((2, 256), dtype=np.float32)
    b2[0, 0:128] = 1.0
    b2[1, 128:256] = 1.0
    c_basis2 = nc.inline_tensor(b2, name="c_basis2").ap()

    r_t = nc.alloc_sbuf_tensor("r_t", [128, C, S], F32)
    a32 = nc.alloc_sbuf_tensor("a32", [128, C, S], BF16)
    vaug = nc.alloc_sbuf_tensor("vaug", [128, NKT, H * 65], BF16)
    rb2 = [nc.alloc_sbuf_tensor(f"rb{i}", [128, G], BF16) for i in range(2)]
    qb2 = [nc.alloc_sbuf_tensor("qb0", [128, G], BF16)]
    qb2 = [qb2[0], qb2[0]]
    tln = nc.alloc_sbuf_tensor("tln", [128, G], F32)
    nmu_row = nc.alloc_sbuf_tensor("nmu_row", [1, G], F32)
    rst_row = nc.alloc_sbuf_tensor("rst_row", [1, G], F32)
    scr_row = nc.alloc_sbuf_tensor("scr_row", [1, G], F32)
    bias_in = nc.alloc_sbuf_tensor("bias_in", [128, 24], F32)
    bias_out = nc.alloc_sbuf_tensor("bias_out", [128, C], F32)
    gam_t = nc.alloc_sbuf_tensor("gam_t", [128, C], F32)
    bet_t = nc.alloc_sbuf_tensor("bet_t", [128, C], F32)
    rw_sb = nc.alloc_sbuf_tensor("rw_sb", [128, C, 16], BF16)
    sel_fx = nc.alloc_sbuf_tensor("sel_fx", [32, 256], BF16)
    idf = nc.alloc_sbuf_tensor("idf", [128, 128], F32)
    idb = nc.alloc_sbuf_tensor("idb", [128, 128], BF16)
    pswap = nc.alloc_sbuf_tensor("pswap", [128, 128], BF16)
    onecol = nc.alloc_sbuf_tensor("onecol", [128, 1], BF16)
    onerow = nc.alloc_sbuf_tensor("onerow", [1, 128], F32)
    one64 = onerow
    hd2_t = nc.alloc_sbuf_tensor("hd2_t", [128, 2], BF16)
    bas2_t = nc.alloc_sbuf_tensor("bas2_t", [2, 256], F32)
    epsb = nc.alloc_sbuf_tensor("epsb", [128, 1], F32)
    qm2 = nc.alloc_sbuf_tensor("qm2", [2, 8], F32)
    km2 = nc.alloc_sbuf_tensor("km2", [2, 8], F32)
    nsh = nc.alloc_sbuf_tensor("nsh", [2, 8], F32)
    nsh2 = nc.alloc_sbuf_tensor("nsh2", [2, 8], F32)
    shbc = nc.alloc_sbuf_tensor("shbc", [128, 16], F32)
    lt_t = nc.alloc_sbuf_tensor("lt_t", [128, 16], F32)
    ex_t = nc.alloc_sbuf_tensor("ex_t", [128, 16], F32)
    eq_t = nc.alloc_sbuf_tensor("eq_t", [128, 16], F32)
    sel_r = nc.alloc_sbuf_tensor("sel_r", [128, 16], F32)
    nm0 = nc.alloc_sbuf_tensor("nm0", [128, 1], F32)
    ssum = nc.alloc_sbuf_tensor("ssum", [128, 1], F32)
    rs_t = nc.alloc_sbuf_tensor("rs_t", [128, 1], F32)
    m1_t = nc.alloc_sbuf_tensor("m1_t", [128, 1], F32)
    m2_t = nc.alloc_sbuf_tensor("m2_t", [128, 1], F32)
    rm2 = nc.alloc_sbuf_tensor("rm2", [2, 1], F32)

    AR = 64 * KB
    arena = nc.alloc_sbuf_tensor("arena", [128, AR], mybir.dt.uint8)
    abase = nc.lookup_mloc(arena).addr

    def at(name, shape, dtype, off):
        return nc.alloc_sbuf_tensor_at(name, shape, dtype, offset=abase + off)

    slot = [at(f"slot{i}", [128, 8192], BF16, i * 16 * KB) for i in range(3)]
    inter = at("inter", [128, 2, FCH, G], BF16, 48 * KB)
    cb_s = at("cb_s", [128, G], F32, 56 * KB)
    lgs = at("lgs", [16, G], F32, 56 * KB)
    sgil = at("sgil", [128, G], F32, 58 * KB)
    coeff = at("coeff", [32, S], BF16, 60 * KB)
    q_rope = at("q_rope", [128, C, S], BF16, 0)
    wi_b = [at(f"wi{i}", [128, C, 128], BF16, 32 * KB + i * 2 * KB) for i in range(2)]
    cosS = at("cosS", [128, S], BF16, 36 * KB)
    sinS = at("sinS", [128, S], BF16, 40 * KB)
    yb = at("yb", [128, C, G], BF16, 44 * KB)
    xn3 = at("xn3", [128, C, G], BF16, 52 * KB)
    qpre = at("qpre", [128, G], BF16, 60 * KB)
    sqb = at("sqb", [128, G], BF16, 61 * KB)
    ys_q = at("ys_q", [128, G], F32, 62 * KB)
    cf_t = at("cf_t", [128, G], F32, 34 * KB)
    ys_l = at("ys_l", [128, G], F32, 34 * KB)
    ptil = at("ptil", [128, NKT, G], BF16, 36 * KB)
    ctxf = at("ctxf", [128, C, G], BF16, 52 * KB)
    wo_b = [at(f"wo{i}", [128, C, 128], BF16, 60 * KB + i * 2 * KB) for i in range(2)]
    stg = [at(f"stg{i}", [128, D], F32, i * 4 * KB) for i in range(2)]
    rowtmp = at("rowtmp", [1, 128], F32, 48 * KB)
    recl_row = at("recl_row", [1, G], F32, 32 * KB)
    p0a = [at(f"p0a{i}", [128, 4096], F32, i * 16 * KB) for i in range(2)]
    p0b = [at(f"p0b{i}", [128, 4096], BF16, 32 * KB + i * 8 * KB) for i in range(2)]

    pball = nc.alloc_psum_tensor("pball", [128, 8, 512], F32)
    pb = [pball[:, i, :] for i in range(8)]

    kb = Kb(nc)
    B = kb.buf
    b_r = B("r")
    b_a32 = B("a32")
    b_vaug = B("vaug")
    b_inter = B("inter")
    b_sgil = B("sgil")
    b_cbs = B("cb_s")
    b_coeff = B("coeff")
    b_rb = [B("rb0"), B("rb1")]
    _bq = B("qb0")
    b_qb = [_bq, _bq]
    b_tln = B("tln")
    b_ys = B("ysq")
    b_ts2 = B("tln")

    b_nmu = B("nmu_row")
    b_rst = B("rst_row")
    b_scr = B("scr_row")
    b_scr2 = B("scr_row2")
    b_lgs = B("lgs")
    b_qpre = B("qpre")
    b_sqb = B("sqb")
    b_cf = B("cf_t")
    b_qrope = B("q_rope")
    b_wi = [B("wi0"), B("wi1")]
    b_cos = B("cosS")
    b_sin = B("sinS")
    b_pt = B("ptil")
    b_yb = B("yb")
    b_ctxf = B("ctxf")
    b_xn3 = B("xn3")
    b_wo = [B("wo0"), B("wo1")]
    b_stg = [B("stg0"), B("stg1")]
    b_p0a = [B("p0a0"), B("p0a1")]
    b_p0b = [B("p0b0"), B("p0b1")]
    b_pb = [B(f"pb{i}") for i in range(8)]
    b_small = B("small")
    b_consts = B("consts")
    b_bias = B("bias")
    b_wc = B("wcache")
    b_qm = B("qm")
    b_km = B("km")
    b_sh = B("sh")
    b_recl = B("recl")
    wcf = [0]

    def ph0():
        if "consts" not in PH0_PARTS:
            return
        kb.dma("sync", idf[:], c_idf[:, :], writes=[b_consts])
        kb.dma("sync", idb[:], c_idb[:, :], writes=[b_consts])
        kb.dma("sync", pswap[:], c_pswap[:, :], writes=[b_consts])
        kb.dma("sync", onecol[:], c_ones_col[:, :], writes=[b_consts])
        kb.dma("sync", onerow[:], c_ones_row[:, :], writes=[b_consts])
        kb.dma("sync", hd2_t[:], c_hd2[:, :], writes=[b_consts])
        kb.dma("sync", bas2_t[:], c_basis2[:, :], writes=[b_consts])
        kb.op("pool", lambda e: e.memset(epsb[:], EPS), writes=[b_consts])
        if "vaug" not in PH0_PARTS:
            return
        for h in range(H):
            kb.op("pool", lambda e, h=h: e.memset(vaug[:, :, h * 65 + 64:h * 65 + 65], 1.0),
                  writes=[b_vaug])

        if "bias" not in PH0_PARTS:
            return
        for src, dst, ncol, bb in ((bi_d, bias_in, 24, b_bias), (bo_d, bias_out, C, b_bias),
                                   (ga_d, gam_t, C, b_bias), (be_d, bet_t, C, b_bias)):
            for c in range(ncol):
                kb.dma("sync", rowtmp[0:1, 0:128], src[c * 128:(c + 1) * 128].unsqueeze(0),
                       writes=[b_small])
                kb.op("pe", lambda e: e.transpose(pb[0][:, 0:1], rowtmp[0:1, 0:128],
                                                  idf[0:1, 0:1]),
                      reads=[b_small, b_consts], writes=[b_pb[0]])
                kb.op("dve", lambda e, c=c, dst=dst: e.tensor_copy(dst[:, c:c + 1],
                                                                   pb[0][:, 0:1]),
                      reads=[b_pb[0]], writes=[bb])

        if "slabs" not in PH0_PARTS:
            return
        slabs = [
            (gw_d.rearrange("e d f -> (e d) f"), wc_g.rearrange("e d f -> (e d) f"), 512, 8),
            (uw_d.rearrange("e d f -> (e d) f"), wc_u.rearrange("e d f -> (e d) f"), 512, 8),
            (dw_d.rearrange("e f d -> (e f) d"), wc_d.rearrange("e f d -> (e f) d"), 1024, 4),
            (sg_d, wc_sg, 1024, 4), (su_d, wc_su, 1024, 4), (sd_d, wc_sd, 1024, 4),
            (Wi_d, wc_wi, 3072, 1), (Wo_d, wc_wo, 1024, 4), (rw_d, wc_rw, 16, 8),
        ]
        for src, dst, cw, a in slabs:
            rows = src.shape[0]
            nsl = rows // (128 * a)
            w = a * cw
            for i in range(nsl):
                bi = i % 2
                sl = slice(i * 128 * a, (i + 1) * 128 * a)
                kb.dma("sync", p0a[bi][:, 0:w].rearrange("p (a f) -> p a f", a=a),
                       src[sl, :].rearrange("(a p) f -> p a f", p=128),
                       writes=[b_p0a[bi]])
                kb.op("pool", lambda e, bi=bi, w=w: e.tensor_copy(p0b[bi][:, 0:w],
                                                                  p0a[bi][:, 0:w]),
                      reads=[b_p0a[bi]], writes=[b_p0b[bi]])
                kb.dma("sync", dst[sl, :].rearrange("(a p) f -> p a f", p=128),
                       p0b[bi][:, 0:w].rearrange("p (a f) -> p a f", a=a),
                       reads=[b_p0b[bi]], writes=[b_wc])
        wcf[0] = kb.cnt["dmaG"]
        if "rw" not in PH0_PARTS:
            return
        kb.dma("sync", rw_sb[:], wc_rw.rearrange("(c p) f -> p c f", p=128),
               reads=[b_wc], writes=[b_consts])

    def wctap():
        kb.dma("sync", a32[:, 0:4, 0:512],
               wc_g[1].rearrange("(kc p) f -> p kc f", p=128)[:, 0:4, :],
               reads=[b_wc], writes=[b_a32])
        kb.dma("sync", a32[:, 4:8, 0:1024],
               wc_sg.rearrange("(kc p) f -> p kc f", p=128)[:, 0:4, :],
               reads=[b_wc], writes=[b_a32])
        for c in range(C):
            kb.op("dve", lambda e, c=c: e.tensor_copy(r_t[:, c, :], a32[:, c, :]),
                  reads=[b_a32], writes=[b_r])
        kb.op("dve", lambda e: e.tensor_copy(r_t[0:32, 0, 0:2048], sel_t[:, 0:2048]),
              reads=[b_consts], writes=[b_r])
        kb.barrier()

    def load(s):
        for tt in range(NKT):
            bi = tt % 2
            kb.dma("sync", stg[bi][:], Xd[s, tt * 128:(tt + 1) * 128, :], writes=[b_stg[bi]])
            for c in range(C):
                j = (tt * C + c) % 6
                kb.op("pe", lambda e, bi=bi, c=c, j=j: e.transpose(
                    pb[j][:, 0:128], stg[bi][:, c * 128:(c + 1) * 128], idf[:]),
                    reads=[b_stg[bi], b_consts], writes=[b_pb[j]])
                kb.op("dve", lambda e, c=c, tt=tt, j=j: e.tensor_copy(
                    r_t[:, c, tt * 128:(tt + 1) * 128], pb[j][:, 0:128]),
                    reads=[b_pb[j]], writes=[b_r])

    def ln_stats(g, get_bf, pbs_i, pbq_i, pba_i, pbb_i):
        for c in range(C):
            rd, ap = get_bf(c)
            qi = c % 2
            kb.op("act", lambda e, ap=ap, qi=qi: e.activation(qb2[qi][:], ap, AF.Square),
                  reads=rd, writes=[b_qb[qi]])
            kb.op("pe", lambda e, ap=ap, c=c: e.matmul(
                pb[pbs_i][0:1, :], onecol[:], ap, start=(c == 0), stop=(c == C - 1)),
                reads=rd + [b_consts], writes=[b_pb[pbs_i]])
            kb.op("pe", lambda e, qi=qi, c=c: e.matmul(
                pb[pbq_i][0:1, :], onecol[:], qb2[qi][:], start=(c == 0), stop=(c == C - 1)),
                reads=[b_qb[qi], b_consts], writes=[b_pb[pbq_i]])
        kb.op("dve", lambda e: e.tensor_single_scalar(nmu_row[:], pb[pbs_i][0:1, :],
                                                      -1.0 / D, ALU.mult),
              reads=[b_pb[pbs_i]], writes=[b_nmu])
        kb.op("dve", lambda e: e.tensor_single_scalar(scr_row[:], pb[pbq_i][0:1, :],
                                                      1.0 / D, ALU.mult),
              reads=[b_pb[pbq_i]], writes=[b_scr])
        kb.op("dve", lambda e: e.tensor_mul(rst_row[:], nmu_row[:], nmu_row[:]),
              reads=[b_nmu], writes=[b_rst])
        kb.op("dve", lambda e: e.tensor_sub(scr_row[:], scr_row[:], rst_row[:]),
              reads=[b_scr, b_rst], writes=[b_scr])
        kb.op("act", lambda e: e.activation(rst_row[:], scr_row[:], AF.Sqrt,
                                            bias=epsb[0:1, :]),
              reads=[b_scr, b_consts], writes=[b_rst])
        kb.op("dve", lambda e: e.reciprocal(rst_row[:], rst_row[:]),
              reads=[b_rst], writes=[b_rst])
        kb.op("pe", lambda e: e.matmul(pb[pba_i], onerow[:], nmu_row[0:1, :],
                                       start=True, stop=True),
              reads=[b_nmu, b_consts], writes=[b_pb[pba_i]])
        kb.op("pe", lambda e: e.matmul(pb[pbb_i], onerow[:], rst_row[0:1, :],
                                       start=True, stop=True),
              reads=[b_rst, b_consts], writes=[b_pb[pbb_i]])

    def r_chunk_bf(g):
        def get(c):
            bi = c % 2
            kb.op("pool", lambda e, c=c, bi=bi: e.tensor_copy(
                rb2[bi][:], r_t[:, c, g * G:(g + 1) * G]),
                reads=[b_r], writes=[b_rb[bi]])
            return [b_rb[bi]], rb2[bi][:]
        return get

    def ffn(s, upto="full"):
        ENTRY = {n: kb.cnt[n] for n in ("pe", "act", "dve", "pool")}
        kb.barrier()
        for g in range(NGRP):
            ln_stats(g, r_chunk_bf(g), 0, 1, 6, 7)
            for c in range(C):
                kb.op("dve", lambda e, c=c, g=g: e.tensor_add(
                    tln[:], r_t[:, c, g * G:(g + 1) * G], pb[6]),
                    reads=[b_r, b_pb[6]], writes=[b_tln])
                kb.op("dve", lambda e, c=c, g=g: e.tensor_mul(
                    a32[:, c, g * G:(g + 1) * G], tln[:], pb[7]),
                    reads=[b_tln, b_pb[7]], writes=[b_a32])
        if upto == "ln":
            for c in range(C):
                for g in range(NGRP):
                    kb.op("dve", lambda e, c=c, g=g: e.tensor_copy(
                        r_t[:, c, g * G:(g + 1) * G], a32[:, c, g * G:(g + 1) * G]),
                        reads=[b_a32], writes=[b_r])
            kb.barrier()
            return
        kb.op("pool", lambda e: e.memset(coeff[:], 1.0), writes=[b_coeff])
        for g in range(NGRP):
            def logits(e, g=g):
                for c in range(C):
                    ins = e.matmul(pb[0][0:16, :], rw_sb[:, c, :],
                                   a32[:, c, g * G:(g + 1) * G],
                                   start=(c == 0), stop=(c == C - 1))
                return ins
            kb.op("pe", logits, reads=[b_a32, b_consts], writes=[b_pb[0]])
            kb.op("dve", lambda e: e.tensor_copy(lgs[:], pb[0][0:16, :]),
                  reads=[b_pb[0]], writes=[b_lgs])
            for tt in range(4):
                col = g * G + tt * 128
                kb.op("pe", lambda e, tt=tt: e.transpose(
                    pb[1][:, 0:16], lgs[0:16, tt * 128:(tt + 1) * 128], idf[0:16, 0:16]),
                    reads=[b_lgs, b_consts], writes=[b_pb[1]])
                kb.op("dve", lambda e: e.tensor_copy(lt_t[:], pb[1][:, 0:16]),
                      reads=[b_pb[1]], writes=[b_small])
                kb.op("dve", lambda e: e.reduce_max(nm0[:], lt_t[:], axis=AX.X, negate=True),
                      reads=[b_small], writes=[b_small])
                kb.op("act", lambda e: e.activation(ex_t[:], lt_t[:], AF.Exp, bias=nm0[:, :]),
                      reads=[b_small], writes=[b_small])
                kb.op("dve", lambda e: e.reduce_sum(ssum[:], ex_t[:], axis=AX.X),
                      reads=[b_small], writes=[b_small])
                kb.op("dve", lambda e: e.reciprocal(rs_t[:], ssum[:]),
                      reads=[b_small], writes=[b_small])
                kb.op("dve", lambda e: e.reduce_max(m1_t[:], ex_t[:], axis=AX.X),
                      reads=[b_small], writes=[b_small])
                kb.op("dve", lambda e: e.tensor_single_scalar(eq_t[:], ex_t[:], m1_t[:, :],
                                                              ALU.is_equal),
                      reads=[b_small], writes=[b_small])
                kb.op("dve", lambda e: e.scalar_tensor_tensor(eq_t[:], eq_t[:], -1e30,
                                                              ex_t[:], ALU.mult, ALU.add),
                      reads=[b_small], writes=[b_small])
                kb.op("dve", lambda e: e.reduce_max(m2_t[:], eq_t[:], axis=AX.X),
                      reads=[b_small], writes=[b_small])
                kb.op("dve", lambda e: e.tensor_single_scalar(sel_r[:], ex_t[:], m2_t[:, :],
                                                              ALU.is_ge),
                      reads=[b_small], writes=[b_small])
                kb.op("dve", lambda e: e.tensor_mul(ex_t[:], ex_t[:], sel_r[:]),
                      reads=[b_small], writes=[b_small])
                kb.op("dve", lambda e: e.tensor_single_scalar(ex_t[:], ex_t[:], rs_t[:, :],
                                                              ALU.mult),
                      reads=[b_small], writes=[b_small])
                kb.op("pe", lambda e: e.transpose(pb[2][0:16, 0:128], ex_t[:], idf[:]),
                      reads=[b_small, b_consts], writes=[b_pb[2]])
                kb.op("dve", lambda e, col=col: e.tensor_copy(
                    coeff[0:16, col:col + 128], pb[2][0:16, 0:128]),
                    reads=[b_pb[2]], writes=[b_coeff])
        if upto == "router":
            kb.op("dve", lambda e: e.tensor_copy(r_t[0:32, 0, :], coeff[:, :]),
                  reads=[b_coeff], writes=[b_r])
            kb.barrier()
            return
        kb.barrier()

        ffn_loop_body(*ffn_wload(ENTRY))
        if upto == "sgil":
            kb.op("dve", lambda e: e.tensor_copy(r_t[:, 0, 0:G], sgil[:, :]),
                  reads=[b_sgil], writes=[b_r])
            kb.op("dve", lambda e: e.tensor_copy(r_t[:, 4, 0:G], cb_s[:]),
                  reads=[b_cbs], writes=[b_r])
            kb.barrier()
            return
        if upto == "inter":
            for ei in range(2):
                for fc in range(FCH):
                    kb.op("dve", lambda e, ei=ei, fc=fc: e.tensor_copy(
                        r_t[:, ei * 4 + fc, 0:G], inter[:, ei, fc, :]),
                        reads=[b_inter], writes=[b_r])
            kb.barrier()


    def ffn_wload(ENTRY):
        P0 = kb.cnt["pe"]
        A0 = kb.cnt["act"]
        D0 = kb.cnt["dve"]
        WSL = (kb.cnt["wA"], kb.cnt["wB"], kb.cnt["wC"])
        WNM = ("wA", "wB", "wC")
        sems = kb.sem
        pe, act, dve, pool, sync = (kb.e["pe"], kb.e["act"], kb.e["dve"],
                                    kb.e["pool"], kb.e["sync"])

        def gu_src(e):
            if e < 16:
                return (wc_g[e].rearrange("(kc p) f -> p kc f", p=128),
                        wc_u[e].rearrange("(kc p) f -> p kc f", p=128))
            h0 = (e - 16) * 512
            return (wc_sg[:, h0:h0 + 512].rearrange("(kc p) f -> p kc f", p=128),
                    wc_su[:, h0:h0 + 512].rearrange("(kc p) f -> p kc f", p=128))

        def d_src(e):
            if e < 16:
                return wc_d[e].rearrange("(kc p) f -> p kc f", p=128)
            h0 = (e - 16) * 512
            return wc_sd[h0:h0 + 512, :].rearrange("(kc p) f -> p kc f", p=128)

        NP = NE // 2
        for _sn in ("d0", "d1", "d2", "d3", "d4", "d5"):
            if kb.cnt[_sn] > 0 and kb.hwm["sync"][_sn] < kb.cnt[_sn]:
                sync.wait_ge(sems[_sn], kb.cnt[_sn])
                kb.hwm["sync"][_sn] = kb.cnt[_sn]
        for e_n, e_v in ENTRY.items():
            if e_v > 0:
                sync.wait_ge(sems[e_n], e_v)
        for j in range(3 * NP):
            p3, r3 = divmod(j, 3)
            sl = slot[r3]
            if p3 >= 1:
                pm = p3 - 1
                if r3 == 0:
                    sync.wait_ge(sems["pe"], P0 + pm * 104 + 88)
                elif r3 == 1:
                    sync.wait_ge(sems["pe"], P0 + pm * 104 + 96)
                else:
                    sync.wait_ge(sems["pe"], P0 + pm * 104 + 104)
            wsem = WNM[r3]
            if r3 < 2:
                e_g = 2 * p3 + r3
                sga, sua = gu_src(e_g)
                sync.dma_start(out=sl[:, 0:4096].rearrange("p (kc f) -> p kc f", kc=8),
                               in_=sga).then_inc(sems[wsem], 16)
                sync.dma_start(out=sl[:, 4096:8192].rearrange("p (kc f) -> p kc f", kc=8),
                               in_=sua).then_inc(sems[wsem], 16)
                if r3 == 0:
                    sync.dma_start(out=sel_fx[:, :],
                                   in_=c_sel[:, 2 * p3 * 128:(2 * p3 + 2) * 128]
                                   ).then_inc(sems[wsem], 16)
                    kb.cnt[wsem] += 16
            else:
                sync.dma_start(out=sl[:, 0:4096].rearrange("p (kc f) -> p kc f", kc=4),
                               in_=d_src(2 * p3)).then_inc(sems[wsem], 16)
                sync.dma_start(out=sl[:, 4096:8192].rearrange("p (kc f) -> p kc f", kc=4),
                               in_=d_src(2 * p3 + 1)).then_inc(sems[wsem], 16)
            kb.cnt[wsem] += 32

        return P0, A0, D0, WSL, WNM

    def ffn_loop_body(P0, A0, D0, WSL, WNM):
        sems = kb.sem
        pe, act, dve = kb.e["pe"], kb.e["act"], kb.e["dve"]
        NP = NE // 2
        for p in range(NP):
            for g in range(NGRP):
                pg = p * 4 + g
                base = P0 + p * 104 + g * 26
                Dg = D0 + pg * 26
                if pg >= 1:
                    pe.wait_ge(sems["dve"], Dg)
                for ei in range(2):
                    mm = pg * 2 + ei
                    st = base + ei * 9
                    sl = slot[ei]
                    if ei == 1:
                        pe.wait_ge(sems["dve"], Dg + 1)
                    if g == 0:
                        mult = 48 if ei == 0 else 32
                        pe.wait_ge(sems[WNM[ei]], WSL[ei] + mult * (p + 1))
                        if ei == 1:
                            pe.wait_ge(sems[WNM[0]], WSL[0] + 48 * (p + 1))
                    pe.matmul(pb[4], sel_fx[:, ei * 128:(ei + 1) * 128],
                              coeff[:, g * G:(g + 1) * G], start=True, stop=True
                              ).then_inc(sems["pe"], 1)
                    for fc in range(FCH):
                        if ei == 1:
                            pe.wait_ge(sems["dve"], Dg + 3 + 2 * fc)
                        elif pg >= 1 and ei == 0:
                            pe.wait_ge(sems["dve"], D0 + (pg - 1) * 26 + 9 + 3 + 2 * fc)
                        for kc in range(C):
                            ins = pe.matmul(
                                pb[fc],
                                sl[:, kc * 512 + fc * 128:kc * 512 + (fc + 1) * 128],
                                a32[:, kc, g * G:(g + 1) * G],
                                start=(kc == 0), stop=(kc == C - 1))
                        ins.then_inc(sems["pe"], 1)
                    for fc in range(FCH):
                        pe.wait_ge(sems["act"], A0 + mm * 4 + fc + 1)
                        for kc in range(C):
                            ins = pe.matmul(
                                pb[fc],
                                sl[:, 4096 + kc * 512 + fc * 128:
                                   4096 + kc * 512 + (fc + 1) * 128],
                                a32[:, kc, g * G:(g + 1) * G],
                                start=(kc == 0), stop=(kc == C - 1))
                        ins.then_inc(sems["pe"], 1)
                pe.wait_ge(sems["dve"], Dg + 18)
                if g == 0:
                    jj = 3 * p + 2
                    pe.wait_ge(sems[WNM[2]], WSL[2] + 32 * (p + 1))
                dsl = slot[2]
                for fo in range(8):
                    first = True
                    for ei in range(2):
                        for kc in range(FCH):
                            ins = pe.matmul(
                                pb[fo],
                                dsl[:, ei * 4096 + kc * 1024 + fo * 128:
                                    ei * 4096 + kc * 1024 + (fo + 1) * 128],
                                inter[:, ei, kc, :],
                                start=first, stop=(ei == 1 and kc == FCH - 1))
                            first = False
                    ins.then_inc(sems["pe"], 1)
        kb.cnt["pe"] += NP * 104

        for mm in range(8 * NP):
            pg, ei = divmod(mm, 2)
            p, g = divmod(pg, 4)
            st = P0 + p * 104 + g * 26 + ei * 9
            for fc in range(FCH):
                act.wait_ge(sems["pe"], st + 2 + fc)
                if fc >= 1:
                    act.wait_ge(sems["dve"], D0 + pg * 26 + ei * 9 + 1 + 2 * fc)
                elif mm >= 1:
                    pgm, eim = divmod(mm - 1, 2)
                    act.wait_ge(sems["dve"], D0 + pgm * 26 + eim * 9 + 9)
                act.activation(sgil[:, :], pb[fc], AF.Silu).then_inc(sems["act"], 1)
        kb.cnt["act"] += 8 * NP * 4

        dcnt = D0
        for p in range(NP):
            for g in range(NGRP):
                pg = p * 4 + g
                for ei in range(2):
                    mm = pg * 2 + ei
                    st = P0 + p * 104 + g * 26 + ei * 9
                    if ei == 0:
                        dve.wait_ge(sems["pe"], P0 + p * 104 + g * 26)
                    dve.wait_ge(sems["pe"], st + 1)
                    if mm >= 1:
                        dve.wait_ge(sems["dve"], dcnt)
                    dve.tensor_copy(cb_s[:], pb[4]).then_inc(sems["dve"], 1)
                    dcnt += 1
                    for fc in range(FCH):
                        dve.wait_ge(sems["act"], A0 + mm * 4 + fc + 1)
                        dve.wait_ge(sems["dve"], dcnt)
                        dve.tensor_mul(sgil[:, :], sgil[:, :],
                                       cb_s[:]).then_inc(sems["dve"], 1)
                        dcnt += 1
                        dve.wait_ge(sems["pe"], st + 6 + fc)
                        dve.wait_ge(sems["dve"], dcnt)
                        dve.tensor_mul(inter[:, ei, fc, :],
                                       sgil[:, :], pb[fc]).then_inc(sems["dve"], 1)
                        dcnt += 1
                for fo in range(8):
                    dve.wait_ge(sems["pe"], P0 + p * 104 + g * 26 + 18 + fo + 1)
                    dve.scalar_tensor_tensor(r_t[:, fo, g * G:(g + 1) * G],
                                             pb[fo], 0.5,
                                             r_t[:, fo, g * G:(g + 1) * G],
                                             ALU.mult, ALU.add).then_inc(sems["dve"], 1)
                    dcnt += 1
        kb.cnt["dve"] += NP * 104
        kb.barrier()
    def attn(s):
        kb.barrier()
        kb.sync_join()
        kb.op("pool", lambda e: e.memset(qm2[:], 0.0), writes=[b_qm])
        kb.op("pool", lambda e: e.memset(km2[:], 0.0), writes=[b_km])
        kb.dma("sync", cosS[:], c_cos[:, :], writes=[b_cos])
        kb.dma("sync", sinS[:], c_sin[:, :], writes=[b_sin])

        def yb_chunk(c):
            return [b_yb], yb[:, c, :]

        for g in range(NGRP):
            gsl = slice(g * G, (g + 1) * G)
            ln_stats(g, r_chunk_bf(g), 0, 1, 6, 7)
            for c in range(C):
                kb.op("dve", lambda e, c=c, gsl=gsl: e.tensor_add(
                    tln[:], r_t[:, c, gsl], pb[6]),
                    reads=[b_r, b_pb[6]], writes=[b_tln])
                kb.op("dve", lambda e, c=c: e.tensor_mul(yb[:, c, :], tln[:], pb[7]),
                      reads=[b_tln, b_pb[7]], writes=[b_yb])
            ln_stats(g, yb_chunk, 0, 1, 4, 5)
            for c in range(C):
                kb.op("dve", lambda e, c=c: e.tensor_add(tln[:], yb[:, c, :], pb[4]),
                      reads=[b_yb, b_pb[4]], writes=[b_tln])
                kb.op("dve", lambda e, c=c: e.tensor_mul(ys_q[:], tln[:], pb[5][:, :]),
                      reads=[b_tln, b_pb[5]], writes=[b_ys])
                kb.op("dve", lambda e, c=c: e.tensor_scalar(
                    xn3[:, c, :], ys_q[:], gam_t[:, c:c + 1], bet_t[:, c:c + 1],
                    ALU.mult, ALU.add),
                    reads=[b_ys, b_bias], writes=[b_xn3])
            for fo in range(24):
                wbuf = fo % 2
                kb.dma("sync", wi_b[wbuf][:],
                       wc_wi[:, fo * 128:(fo + 1) * 128].rearrange("(c p) f -> p c f", p=128),
                       reads=[b_wc], writes=[b_wi[wbuf]])
                pj = fo % 3
                def qkvmm(e, wbuf=wbuf, pj=pj):
                    for c in range(C):
                        ins = e.matmul(pb[pj], wi_b[wbuf][:, c, :], xn3[:, c, :],
                                       start=(c == 0), stop=(c == C - 1))
                    return ins
                kb.op("pe", qkvmm, reads=[b_wi[wbuf], b_xn3], writes=[b_pb[pj]])
                if fo < 16:
                    isq = fo < 8
                    dst = q_rope if isq else a32
                    dstb = b_qrope if isq else b_a32
                    cc = fo if isq else fo - 8
                    kb.op("dve", lambda e, pj=pj, fo=fo: e.tensor_single_scalar(
                        qpre[:], pb[pj], bias_in[:, fo:fo + 1], ALU.add),
                        reads=[b_pb[pj], b_bias], writes=[b_qpre])
                    kb.op("pe", lambda e: e.matmul(pb[3][:, :], pswap[:], qpre[:],
                                                   start=True, stop=True),
                          reads=[b_qpre, b_consts], writes=[b_pb[3]])
                    kb.op("dve", lambda e, gsl=gsl: e.tensor_mul(tln[:], qpre[:],
                                                                 cosS[:, gsl]),
                          reads=[b_qpre, b_cos], writes=[b_tln])
                    kb.op("dve", lambda e, gsl=gsl: e.tensor_mul(ys_q[:], pb[3][:, :],
                                                                 sinS[:, gsl]),
                          reads=[b_pb[3], b_sin], writes=[b_ys])
                    kb.op("dve", lambda e, dst=dst, cc=cc, gsl=gsl: e.tensor_add(
                        dst[:, cc, gsl], tln[:], ys_q[:]),
                        reads=[b_tln, b_ys], writes=[dstb])
                    kb.op("act", lambda e, dst=dst, cc=cc, gsl=gsl: e.activation(
                        sqb[:], dst[:, cc, gsl], AF.Square),
                        reads=[dstb], writes=[b_sqb])
                    kb.op("pe", lambda e: e.matmul(pb[4][0:2, :], hd2_t[:], sqb[:],
                                                   start=True, stop=True),
                          reads=[b_sqb, b_consts], writes=[b_pb[4]])
                    kb.op("dve", lambda e: e.reduce_max(rm2[:], pb[4][0:2, :], axis=AX.X),
                          reads=[b_pb[4]], writes=[b_small])
                    mt = qm2 if isq else km2
                    mb = b_qm if isq else b_km
                    kb.op("dve", lambda e, mt=mt, cc=cc: e.tensor_max(
                        mt[0:2, cc:cc + 1], mt[0:2, cc:cc + 1], rm2[:]),
                        reads=[b_small], writes=[mb])
                else:
                    vc = fo - 16
                    kb.op("dve", lambda e, pj=pj, fo=fo: e.tensor_single_scalar(
                        qpre[:], pb[pj], bias_in[:, fo:fo + 1], ALU.add),
                        reads=[b_pb[pj], b_bias], writes=[b_qpre])
                    for t4 in range(4):
                        kb.op("pe", lambda e, t4=t4: e.transpose(
                            pb[5].bitcast(BF16)[:, 0:128],
                            qpre[:, t4 * 128:(t4 + 1) * 128], idb[:]),
                            reads=[b_qpre, b_consts], writes=[b_pb[5]])
                        kt = g * 4 + t4
                        hA = 2 * vc
                        kb.op("dve", lambda e, kt=kt, hA=hA: e.tensor_copy(
                            vaug[:, kt, hA * 65:hA * 65 + 64],
                            pb[5].bitcast(BF16)[:, 0:64]),
                            reads=[b_pb[5]], writes=[b_vaug])
                        kb.op("dve", lambda e, kt=kt, hA=hA: e.tensor_copy(
                            vaug[:, kt, (hA + 1) * 65:(hA + 1) * 65 + 64],
                            pb[5].bitcast(BF16)[:, 64:128]),
                            reads=[b_pb[5]], writes=[b_vaug])

        kb.op("dve", lambda e: e.tensor_mul(nsh[:], qm2[:], km2[:]),
              reads=[b_qm, b_km], writes=[b_sh])
        kb.op("act", lambda e: e.activation(nsh2[:], nsh[:], AF.Sqrt),
              reads=[b_sh], writes=[b_sh])
        kb.op("dve", lambda e: e.tensor_single_scalar(nsh[:], nsh2[:], -0.125, ALU.mult),
              reads=[b_sh], writes=[b_sh])
        for rr in range(2):
            kb.op("pe", lambda e, rr=rr: e.matmul(
                pb[6][:, rr * 8:(rr + 1) * 8], bas2_t[:, rr * 128:(rr + 1) * 128],
                nsh[0:2, 0:8], start=True, stop=True),
                reads=[b_sh, b_consts], writes=[b_pb[6]])
        kb.op("dve", lambda e: e.tensor_copy(shbc[:], pb[6][:, 0:16]),
              reads=[b_pb[6]], writes=[b_sh])
        kb.barrier()
        kb.sync_join()

        for qg in range(NGRP):
            qsl = slice(qg * G, (qg + 1) * G)
            for h in range(H):
                c2 = h // 2
                ro = (h % 2) * 64
                jcol = (h % 2) * 8 + c2
                pc = 4 + (h % 2)
                for k2 in range(8):
                    bk = (k2 % 2) * 2
                    for kk in range(2):
                        kt = k2 * 2 + kk
                        kb.op("pe", lambda e, bk=bk, kk=kk, c2=c2, ro=ro, kt=kt, qsl=qsl:
                              e.matmul(pb[bk + kk],
                                       a32[ro:ro + 64, c2, kt * 128:(kt + 1) * 128],
                                       q_rope[ro:ro + 64, c2, qsl],
                                       start=True, stop=True),
                              reads=[b_a32, b_qrope], writes=[b_pb[bk + kk]])
                    kb.op("act", lambda e, k2=k2, bk=bk, jcol=jcol: e.activation(
                        ptil[:, k2 * 2:(k2 + 1) * 2, :], pball[:, bk:bk + 2, :], AF.Exp,
                        bias=shbc[:, jcol:jcol + 1], scale=0.125),
                        reads=[b_pb[bk], b_pb[bk + 1], b_sh], writes=[b_pt])
                    def ctxmm(e, pc=pc, h=h, k2=k2):
                        for kk in range(2):
                            kt = k2 * 2 + kk
                            ins = e.matmul(pb[pc][0:65, :], vaug[:, kt, h * 65:h * 65 + 65],
                                           ptil[:, kt, :], start=(kt == 0),
                                           stop=(kt == NKT - 1))
                        return ins
                    kb.op("pe", ctxmm, reads=[b_pt, b_vaug], writes=[b_pb[pc]])
                kb.op("dve", lambda e, pc=pc: e.reciprocal(recl_row[:], pb[pc][64:65, :]),
                      reads=[b_pb[pc]], writes=[b_recl])
                kb.op("pe", lambda e: e.matmul(pb[7][0:64, :], onerow[0:1, 0:64],
                                               recl_row[0:1, :], start=True, stop=True),
                      reads=[b_recl, b_consts], writes=[b_pb[7]])
                kb.op("act", lambda e, pc=pc: e.activation(cf_t[0:64, :], pb[pc][0:64, :],
                                                           AF.Copy),
                      reads=[b_pb[pc]], writes=[b_cf])
                kb.op("dve", lambda e, c2=c2, ro=ro: e.tensor_mul(
                    ctxf[ro:ro + 64, c2, :], cf_t[0:64, :], pb[7][0:64, :]),
                    reads=[b_cf, b_pb[7]], writes=[b_ctxf])
            ln_stats(qg, r_chunk_bf(qg), 0, 1, 2, 3)
            for fo in range(C):
                wbuf = fo % 2
                kb.dma("sync", wo_b[wbuf][:],
                       wc_wo[:, fo * 128:(fo + 1) * 128].rearrange("(c p) f -> p c f", p=128),
                       reads=[b_wc], writes=[b_wo[wbuf]])
                pj = fo % 2 + 6
                def projmm(e, wbuf=wbuf, pj=pj):
                    for c in range(C):
                        ins = e.matmul(pb[pj], wo_b[wbuf][:, c, :], ctxf[:, c, :],
                                       start=(c == 0), stop=(c == C - 1))
                    return ins
                kb.op("pe", projmm, reads=[b_wo[wbuf], b_ctxf], writes=[b_pb[pj]])
                kb.op("dve", lambda e, fo=fo, qsl=qsl: e.tensor_add(
                    tln[:], r_t[:, fo, qsl], pb[2][:, :]),
                    reads=[b_r, b_pb[2]], writes=[b_tln])
                kb.op("dve", lambda e: e.tensor_mul(ys_l[:], tln[:], pb[3][:, :]),
                      reads=[b_tln, b_pb[3]], writes=[b_cf])
                kb.op("dve", lambda e, pj=pj, fo=fo: e.scalar_tensor_tensor(
                    tln[:], pb[pj], bias_out[:, fo:fo + 1], ys_l[:], ALU.add, ALU.add),
                    reads=[b_pb[pj], b_cf, b_bias], writes=[b_tln])
                kb.op("dve", lambda e, fo=fo, qsl=qsl: e.tensor_add(
                    r_t[:, fo, qsl], r_t[:, fo, qsl], tln[:]),
                    reads=[b_tln], writes=[b_r])
        kb.barrier()

    def store(s):
        for tt in range(NKT):
            bi = tt % 2
            for c in range(C):
                j = (tt * C + c) % 6
                kb.op("pe", lambda e, c=c, tt=tt, j=j: e.transpose(
                    pb[j][:, 0:128], r_t[:, c, tt * 128:(tt + 1) * 128], idf[:]),
                    reads=[b_r, b_consts], writes=[b_pb[j]])
                kb.op("dve", lambda e, c=c, bi=bi, j=j: e.tensor_copy(
                    stg[bi][:, c * 128:(c + 1) * 128], pb[j][:, 0:128]),
                    reads=[b_pb[j]], writes=[b_stg[bi]])
            kb.dma("sync", OUT[s, tt * 128:(tt + 1) * 128, :], stg[bi][:],
                   reads=[b_stg[bi]])

    from contextlib import nullcontext
    def scope(name):
        return nc.named_scope(name)
    with scope("ph0"):
        ph0()
    for s in range(nseq):
        kb.barrier()
        if stage == "wc":
            wctap()
            store(s)
            continue
        with scope(f"load{s}"):
            load(s)
        if stage != "load":
            with scope(f"ffnA{s}"):
                ffn(s, upto=(stage if stage in ("ln", "router", "inter", "sgil") else "full"))
        if stage in ("attn", "full"):
            with scope(f"attn{s}"):
                attn(s)
        if stage == "full":
            with scope(f"ffnB{s}"):
                ffn(s)
        kb.barrier()
        with scope(f"store{s}"):
            store(s)
    for _sn in ("d0", "d1", "d2", "d3", "d4", "d5", "g0", "g1",
                "wA", "wB", "wC"):
        if kb.cnt[_sn] > 0:
            kb.e["sync"].wait_ge(kb.sem[_sn], kb.cnt[_sn])
    return nc


WNAMES = ["L0_attn_W_in", "L0_attn_b_in", "L0_attn_W_out", "L0_attn_b_out",
          "L0_attn_gamma", "L0_attn_beta", "L0_ffn_router_w", "L0_ffn_gate_w",
          "L0_ffn_up_w", "L0_ffn_down_w", "L0_ffn_shared_gate_w",
          "L0_ffn_shared_up_w", "L0_ffn_shared_down_w"]


def kernel(**inputs):
    X = np.ascontiguousarray(np.asarray(inputs["L0_attn_X"], dtype=np.float32))
    weights = {n: np.ascontiguousarray(np.asarray(inputs[n], dtype=np.float32))
               for n in WNAMES}
    shards = np.split(X, 8, axis=0)
    nc = build_nc(nseq=4, stage="full")
    in_maps = [dict(weights, L0_attn_X=shards[i]) for i in range(8)]
    res = run_bass_kernel_spmd(nc, in_maps, core_ids=list(range(8)))
    return np.concatenate([r["OUT"] for r in res.results], axis=0)


# version 06 (final)
# speedup vs baseline: 1.2843x; 1.2843x over previous
import numpy as np
import ml_dtypes
import concourse.bass as bass
import concourse.mybir as mybir
from concourse.bass_utils import run_bass_kernel_spmd

F32 = mybir.dt.float32
BF16 = mybir.dt.bfloat16
AF = mybir.ActivationFunctionType
ALU = mybir.AluOpType
AX = mybir.AxisListType
bfdt = ml_dtypes.bfloat16

PH0_PARTS = ("consts", "vaug", "bias", "slabs", "rw")

S = 2048
G = 512
NGRP = 4
D = 1024
C = 8
NE = 18
F = 512
FCH = 4
H = 16
DH = 64
NKT = 16
EPS = 1e-5
KB = 1024


class Buf:
    __slots__ = ("w", "r", "name")

    def __init__(self, name):
        self.name = name
        self.w = None
        self.r = {}


class Kb:
    ENGS = ("pe", "act", "dve", "pool", "sync")

    def __init__(self, nc):
        self.nc = nc
        self.e = {"pe": nc.tensor, "act": nc.scalar, "dve": nc.vector,
                  "pool": nc.gpsimd, "sync": nc.sync}
        names = (*self.ENGS, "dmaS", "dmaG", "wA", "wB", "wC",
                 "d0", "d1", "d2", "d3", "d4", "d5", "g0", "g1")
        self.sem = {n: nc.alloc_semaphore("s_" + n) for n in names}
        self.cnt = {n: 0 for n in names}
        self.hwm = {n: {m: 0 for m in names} for n in self.ENGS}
        self.bufs = []
        self.rot = {"sync": ["d0", "d1", "d2", "d3", "d4", "d5"],
                    "pool": ["g0", "g1"]}
        self.rix = {"sync": 0, "pool": 0}

    def buf(self, name):
        b = Buf(name)
        self.bufs.append(b)
        return b

    def _waits(self, eng, deps):
        best = {}
        for d in deps:
            if d is None:
                continue
            s, v = d
            if v and v > best.get(s, 0):
                best[s] = v
        for s, v in best.items():
            if self.hwm[eng][s] < v:
                self.e[eng].wait_ge(self.sem[s], v)
                self.hwm[eng][s] = v

    def op(self, eng, fn, reads=(), writes=(), extra=()):
        deps = list(extra)
        for b in reads:
            deps.append(b.w)
        for b in writes:
            deps.append(b.w)
            deps.extend(b.r.items())
        self._waits(eng, deps)
        ins = fn(self.e[eng])
        ins.then_inc(self.sem[eng], 1)
        self.cnt[eng] += 1
        v = self.cnt[eng]
        for b in reads:
            b.r[eng] = v
        for b in writes:
            b.w = (eng, v)
            b.r = {}
        return v

    def dma(self, q, out, in_, reads=(), writes=(), extra=(), **kw):
        eng = "sync" if q == "sync" else "pool"
        deps = list(extra)
        for b in reads:
            deps.append(b.w)
        for b in writes:
            deps.append(b.w)
            deps.extend(b.r.items())
        self._waits(eng, deps)
        sems = self.rot[eng]
        dsem = sems[self.rix[eng] % len(sems)]
        self.rix[eng] += 1
        if self.cnt[dsem] > 0:
            self._waits(eng, [(dsem, self.cnt[dsem])])
        self.e[eng].dma_start(out=out, in_=in_, **kw).then_inc(self.sem[dsem], 16)
        self.cnt[dsem] += 16
        v = self.cnt[dsem]
        for b in reads:
            b.r[dsem] = v
        for b in writes:
            b.w = (dsem, v)
            b.r = {}
        return v

    BAR_ENGS = ("pe", "act", "dve", "pool")

    def sync_join(self):
        for n in self.BAR_ENGS:
            if self.cnt[n] > 0:
                self.e["sync"].wait_ge(self.sem[n], self.cnt[n])
                self.hwm["sync"][n] = self.cnt[n]

    def barrier(self):
        import concourse.mybir as _mb
        self.nc.multi_engine_barrier([_mb.EngineType.PE, _mb.EngineType.Activation,
                                      _mb.EngineType.DVE, _mb.EngineType.Pool])
        for n in self.BAR_ENGS:
            for m in self.BAR_ENGS:
                self.hwm[n][m] = self.cnt[m]


def _rope_tables():
    inv = 1.0 / (10000.0 ** (np.arange(0, 32, 2, dtype=np.float64) / 32.0))
    t = np.arange(S, dtype=np.float64)
    fr = t[:, None] * inv[None, :]
    fr = np.repeat(fr, 2, axis=1)
    cos = np.ones((128, S), dtype=np.float64)
    sin = np.zeros((128, S), dtype=np.float64)
    for base in (0, 64):
        cos[base:base + 32, :] = np.cos(fr).T
        sin[base:base + 32, :] = np.sin(fr).T
    return cos.astype(bfdt), sin.astype(bfdt)


def _pswap():
    p = np.zeros((128, 128), dtype=np.float32)
    for base in (0, 64):
        for i in range(16):
            p[base + 2 * i + 1, base + 2 * i] = -1.0
            p[base + 2 * i, base + 2 * i + 1] = 1.0
    return p.astype(bfdt)


def build_nc(nseq=4, stage="full"):
    nc = bass.Bass("TRN2", target_bir_lowering=False, dynamic_dma_scratch_size=16384)

    Xd = nc.dram_tensor("L0_attn_X", [nseq, S, D], F32, kind="ExternalInput").ap()
    Wi_d = nc.dram_tensor("L0_attn_W_in", [D, 3 * D], F32, kind="ExternalInput").ap()
    bi_d = nc.dram_tensor("L0_attn_b_in", [3 * D], F32, kind="ExternalInput").ap()
    Wo_d = nc.dram_tensor("L0_attn_W_out", [D, D], F32, kind="ExternalInput").ap()
    bo_d = nc.dram_tensor("L0_attn_b_out", [D], F32, kind="ExternalInput").ap()
    ga_d = nc.dram_tensor("L0_attn_gamma", [D], F32, kind="ExternalInput").ap()
    be_d = nc.dram_tensor("L0_attn_beta", [D], F32, kind="ExternalInput").ap()
    rw_d = nc.dram_tensor("L0_ffn_router_w", [D, 16], F32, kind="ExternalInput").ap()
    gw_d = nc.dram_tensor("L0_ffn_gate_w", [16, D, F], F32, kind="ExternalInput").ap()
    uw_d = nc.dram_tensor("L0_ffn_up_w", [16, D, F], F32, kind="ExternalInput").ap()
    dw_d = nc.dram_tensor("L0_ffn_down_w", [16, F, D], F32, kind="ExternalInput").ap()
    sg_d = nc.dram_tensor("L0_ffn_shared_gate_w", [D, D], F32, kind="ExternalInput").ap()
    su_d = nc.dram_tensor("L0_ffn_shared_up_w", [D, D], F32, kind="ExternalInput").ap()
    sd_d = nc.dram_tensor("L0_ffn_shared_down_w", [D, D], F32, kind="ExternalInput").ap()
    OUT = nc.dram_tensor("OUT", [nseq, S, D], F32, kind="ExternalOutput").ap()


    cosT, sinT = _rope_tables()
    c_cos = nc.inline_tensor(cosT, name="c_cos").ap()
    c_sin = nc.inline_tensor(sinT, name="c_sin").ap()
    c_pswap = nc.inline_tensor(_pswap(), name="c_pswap").ap()
    c_idf = nc.inline_tensor(np.eye(128, dtype=np.float32), name="c_idf").ap()
    c_idb = nc.inline_tensor(np.eye(128, dtype=bfdt), name="c_idb").ap()
    c_ones_col = nc.inline_tensor(np.ones((128, 1), dtype=bfdt), name="c_ones_col").ap()
    c_ones_row = nc.inline_tensor(np.ones((1, 128), dtype=np.float32), name="c_ones_row").ap()
    c_ones64 = nc.inline_tensor(np.ones((1, 64), dtype=np.float32), name="c_ones64").ap()
    hd2 = np.zeros((128, 2), dtype=bfdt)
    hd2[0:64, 0] = 1.0
    hd2[64:128, 1] = 1.0
    c_hd2 = nc.inline_tensor(hd2, name="c_hd2").ap()
    sel = np.zeros((32, NE * 128), dtype=bfdt)
    for e in range(NE):
        sel[e, e * 128:(e + 1) * 128] = 1.0
    c_sel = nc.inline_tensor(sel, name="c_sel").ap()
    b2 = np.zeros((2, 256), dtype=np.float32)
    b2[0, 0:128] = 1.0
    b2[1, 128:256] = 1.0
    c_basis2 = nc.inline_tensor(b2, name="c_basis2").ap()

    r_t = nc.alloc_sbuf_tensor("r_t", [128, C, S], F32)
    a32 = nc.alloc_sbuf_tensor("a32", [128, C, S], BF16)
    vaug = nc.alloc_sbuf_tensor("vaug", [128, NKT, H * 65], BF16)
    rb2 = [nc.alloc_sbuf_tensor(f"rb{i}", [128, G], BF16) for i in range(2)]
    qb2 = [nc.alloc_sbuf_tensor("qb0", [128, G], BF16)]
    qb2 = [qb2[0], qb2[0]]
    tln = nc.alloc_sbuf_tensor("tln", [128, G], F32)
    nmu_row = nc.alloc_sbuf_tensor("nmu_row", [1, G], F32)
    rst_row = nc.alloc_sbuf_tensor("rst_row", [1, G], F32)
    scr_row = nc.alloc_sbuf_tensor("scr_row", [1, G], F32)
    bias_in = nc.alloc_sbuf_tensor("bias_in", [128, 24], F32)
    bias_out = nc.alloc_sbuf_tensor("bias_out", [128, C], F32)
    gam_t = nc.alloc_sbuf_tensor("gam_t", [128, C], F32)
    bet_t = nc.alloc_sbuf_tensor("bet_t", [128, C], F32)
    rw_sb = nc.alloc_sbuf_tensor("rw_sb", [128, C, 16], BF16)
    sel_fx = nc.alloc_sbuf_tensor("sel_fx", [32, 256], BF16)
    idf = nc.alloc_sbuf_tensor("idf", [128, 128], F32)
    idb = nc.alloc_sbuf_tensor("idb", [128, 128], BF16)
    pswap = nc.alloc_sbuf_tensor("pswap", [128, 128], BF16)
    onecol = nc.alloc_sbuf_tensor("onecol", [128, 1], BF16)
    onerow = nc.alloc_sbuf_tensor("onerow", [1, 128], F32)
    one64 = onerow
    hd2_t = nc.alloc_sbuf_tensor("hd2_t", [128, 2], BF16)
    bas2_t = nc.alloc_sbuf_tensor("bas2_t", [2, 256], F32)
    epsb = nc.alloc_sbuf_tensor("epsb", [128, 1], F32)
    qm2 = nc.alloc_sbuf_tensor("qm2", [2, 8], F32)
    km2 = nc.alloc_sbuf_tensor("km2", [2, 8], F32)
    nsh = nc.alloc_sbuf_tensor("nsh", [2, 8], F32)
    nsh2 = nc.alloc_sbuf_tensor("nsh2", [2, 8], F32)
    shbc = nc.alloc_sbuf_tensor("shbc", [128, 16], F32)
    lt_t = nc.alloc_sbuf_tensor("lt_t", [128, 16], F32)
    ex_t = nc.alloc_sbuf_tensor("ex_t", [128, 16], F32)
    eq_t = nc.alloc_sbuf_tensor("eq_t", [128, 16], F32)
    sel_r = nc.alloc_sbuf_tensor("sel_r", [128, 16], F32)
    nm0 = nc.alloc_sbuf_tensor("nm0", [128, 1], F32)
    ssum = nc.alloc_sbuf_tensor("ssum", [128, 1], F32)
    rs_t = nc.alloc_sbuf_tensor("rs_t", [128, 1], F32)
    m1_t = nc.alloc_sbuf_tensor("m1_t", [128, 1], F32)
    m2_t = nc.alloc_sbuf_tensor("m2_t", [128, 1], F32)
    rm2 = nc.alloc_sbuf_tensor("rm2", [2, 1], F32)

    AR = 64 * KB
    arena = nc.alloc_sbuf_tensor("arena", [128, AR], mybir.dt.uint8)
    abase = nc.lookup_mloc(arena).addr

    def at(name, shape, dtype, off):
        return nc.alloc_sbuf_tensor_at(name, shape, dtype, offset=abase + off)

    slot = [at(f"slot{i}", [128, 8192], BF16, i * 16 * KB) for i in range(3)]
    inter = at("inter", [128, 2, FCH, G], BF16, 48 * KB)
    cb_s = at("cb_s", [128, G], F32, 56 * KB)
    lgs = at("lgs", [16, G], F32, 56 * KB)
    sgil = at("sgil", [128, G], F32, 58 * KB)
    coeff = at("coeff", [32, S], BF16, 60 * KB)
    q_rope = at("q_rope", [128, C, S], BF16, 0)
    wi_b = [at(f"wi{i}", [128, C, 128], BF16, 32 * KB + i * 2 * KB) for i in range(2)]
    cosS = at("cosS", [128, S], BF16, 36 * KB)
    sinS = at("sinS", [128, S], BF16, 40 * KB)
    yb = at("yb", [128, C, G], BF16, 44 * KB)
    xn3 = at("xn3", [128, C, G], BF16, 52 * KB)
    qpre = at("qpre", [128, G], BF16, 60 * KB)
    sqb = at("sqb", [128, G], BF16, 61 * KB)
    ys_q = at("ys_q", [128, G], F32, 62 * KB)
    cf_t = at("cf_t", [128, G], F32, 34 * KB)
    ys_l = at("ys_l", [128, G], F32, 34 * KB)
    ptil = at("ptil", [128, NKT, G], BF16, 36 * KB)
    ctxf = at("ctxf", [128, C, G], BF16, 52 * KB)
    wo_b = [at(f"wo{i}", [128, C, 128], BF16, 60 * KB + i * 2 * KB) for i in range(2)]
    stg = [at(f"stg{i}", [128, D], F32, i * 4 * KB) for i in range(2)]
    rowtmp = at("rowtmp", [1, 128], F32, 48 * KB)
    recl_row = at("recl_row", [1, G], F32, 32 * KB)
    p0a = [at(f"p0a{i}", [128, 4096], F32, i * 16 * KB) for i in range(2)]
    p0b = [at(f"p0b{i}", [128, 4096], BF16, 32 * KB + i * 8 * KB) for i in range(2)]

    pball = nc.alloc_psum_tensor("pball", [128, 8, 512], F32)
    pb = [pball[:, i, :] for i in range(8)]

    kb = Kb(nc)
    B = kb.buf
    b_r = B("r")
    b_a32 = B("a32")
    b_vaug = B("vaug")
    b_inter = B("inter")
    b_sgil = B("sgil")
    b_cbs = B("cb_s")
    b_coeff = B("coeff")
    b_rb = [B("rb0"), B("rb1")]
    _bq = B("qb0")
    b_qb = [_bq, _bq]
    b_tln = B("tln")
    b_ys = B("ysq")
    b_ts2 = B("tln")

    b_nmu = B("nmu_row")
    b_rst = B("rst_row")
    b_scr = B("scr_row")
    b_scr2 = B("scr_row2")
    b_lgs = B("lgs")
    b_qpre = B("qpre")
    b_sqb = B("sqb")
    b_cf = B("cf_t")
    b_qrope = B("q_rope")
    b_wi = [B("wi0"), B("wi1")]
    b_cos = B("cosS")
    b_sin = B("sinS")
    b_pt = B("ptil")
    b_yb = B("yb")
    b_ctxf = B("ctxf")
    b_xn3 = B("xn3")
    b_wo = [B("wo0"), B("wo1")]
    b_stg = [B("stg0"), B("stg1")]
    b_p0a = [B("p0a0"), B("p0a1")]
    b_p0b = [B("p0b0"), B("p0b1")]
    b_pb = [B(f"pb{i}") for i in range(8)]
    b_small = B("small")
    b_consts = B("consts")
    b_bias = B("bias")
    b_wc = B("wcache")
    b_qm = B("qm")
    b_km = B("km")
    b_sh = B("sh")
    b_recl = B("recl")
    wcf = [0]

    def ph0():
        if "consts" not in PH0_PARTS:
            return
        kb.dma("sync", idf[:], c_idf[:, :], writes=[b_consts])
        kb.dma("sync", idb[:], c_idb[:, :], writes=[b_consts])
        kb.dma("sync", pswap[:], c_pswap[:, :], writes=[b_consts])
        kb.dma("sync", onecol[:], c_ones_col[:, :], writes=[b_consts])
        kb.dma("sync", onerow[:], c_ones_row[:, :], writes=[b_consts])
        kb.dma("sync", hd2_t[:], c_hd2[:, :], writes=[b_consts])
        kb.dma("sync", bas2_t[:], c_basis2[:, :], writes=[b_consts])
        kb.op("pool", lambda e: e.memset(epsb[:], EPS), writes=[b_consts])
        if "vaug" not in PH0_PARTS:
            return
        for h in range(H):
            kb.op("pool", lambda e, h=h: e.memset(vaug[:, :, h * 65 + 64:h * 65 + 65], 1.0),
                  writes=[b_vaug])

        if "bias" not in PH0_PARTS:
            return
        for src, dst, ncol, bb in ((bi_d, bias_in, 24, b_bias), (bo_d, bias_out, C, b_bias),
                                   (ga_d, gam_t, C, b_bias), (be_d, bet_t, C, b_bias)):
            for c in range(ncol):
                kb.dma("sync", rowtmp[0:1, 0:128], src[c * 128:(c + 1) * 128].unsqueeze(0),
                       writes=[b_small])
                kb.op("pe", lambda e: e.transpose(pb[0][:, 0:1], rowtmp[0:1, 0:128],
                                                  idf[0:1, 0:1]),
                      reads=[b_small, b_consts], writes=[b_pb[0]])
                kb.op("dve", lambda e, c=c, dst=dst: e.tensor_copy(dst[:, c:c + 1],
                                                                   pb[0][:, 0:1]),
                      reads=[b_pb[0]], writes=[bb])

        if "rw" not in PH0_PARTS:
            return
        kb.dma("pool", rw_sb[:], rw_d.rearrange("(c p) f -> p c f", p=128),
               writes=[b_consts])

    def wctap():
        kb.dma("sync", a32[:, 0:4, 0:512],
               wc_g[1].rearrange("(kc p) f -> p kc f", p=128)[:, 0:4, :],
               reads=[b_wc], writes=[b_a32])
        kb.dma("sync", a32[:, 4:8, 0:1024],
               wc_sg.rearrange("(kc p) f -> p kc f", p=128)[:, 0:4, :],
               reads=[b_wc], writes=[b_a32])
        for c in range(C):
            kb.op("dve", lambda e, c=c: e.tensor_copy(r_t[:, c, :], a32[:, c, :]),
                  reads=[b_a32], writes=[b_r])
        kb.op("dve", lambda e: e.tensor_copy(r_t[0:32, 0, 0:2048], sel_t[:, 0:2048]),
              reads=[b_consts], writes=[b_r])
        kb.barrier()

    def load(s):
        for tt in range(NKT):
            bi = tt % 2
            kb.dma("sync", stg[bi][:], Xd[s, tt * 128:(tt + 1) * 128, :], writes=[b_stg[bi]])
            for c in range(C):
                j = (tt * C + c) % 6
                kb.op("pe", lambda e, bi=bi, c=c, j=j: e.transpose(
                    pb[j][:, 0:128], stg[bi][:, c * 128:(c + 1) * 128], idf[:]),
                    reads=[b_stg[bi], b_consts], writes=[b_pb[j]])
                kb.op("dve", lambda e, c=c, tt=tt, j=j: e.tensor_copy(
                    r_t[:, c, tt * 128:(tt + 1) * 128], pb[j][:, 0:128]),
                    reads=[b_pb[j]], writes=[b_r])

    def ln_stats(g, get_bf, pbs_i, pbq_i, pba_i, pbb_i):
        for c in range(C):
            rd, ap = get_bf(c)
            qi = c % 2
            kb.op("act", lambda e, ap=ap, qi=qi: e.activation(qb2[qi][:], ap, AF.Square),
                  reads=rd, writes=[b_qb[qi]])
            kb.op("pe", lambda e, ap=ap, c=c: e.matmul(
                pb[pbs_i][0:1, :], onecol[:], ap, start=(c == 0), stop=(c == C - 1)),
                reads=rd + [b_consts], writes=[b_pb[pbs_i]])
            kb.op("pe", lambda e, qi=qi, c=c: e.matmul(
                pb[pbq_i][0:1, :], onecol[:], qb2[qi][:], start=(c == 0), stop=(c == C - 1)),
                reads=[b_qb[qi], b_consts], writes=[b_pb[pbq_i]])
        kb.op("dve", lambda e: e.tensor_single_scalar(nmu_row[:], pb[pbs_i][0:1, :],
                                                      -1.0 / D, ALU.mult),
              reads=[b_pb[pbs_i]], writes=[b_nmu])
        kb.op("dve", lambda e: e.tensor_single_scalar(scr_row[:], pb[pbq_i][0:1, :],
                                                      1.0 / D, ALU.mult),
              reads=[b_pb[pbq_i]], writes=[b_scr])
        kb.op("dve", lambda e: e.tensor_mul(rst_row[:], nmu_row[:], nmu_row[:]),
              reads=[b_nmu], writes=[b_rst])
        kb.op("dve", lambda e: e.tensor_sub(scr_row[:], scr_row[:], rst_row[:]),
              reads=[b_scr, b_rst], writes=[b_scr])
        kb.op("act", lambda e: e.activation(rst_row[:], scr_row[:], AF.Sqrt,
                                            bias=epsb[0:1, :]),
              reads=[b_scr, b_consts], writes=[b_rst])
        kb.op("dve", lambda e: e.reciprocal(rst_row[:], rst_row[:]),
              reads=[b_rst], writes=[b_rst])
        kb.op("pe", lambda e: e.matmul(pb[pba_i], onerow[:], nmu_row[0:1, :],
                                       start=True, stop=True),
              reads=[b_nmu, b_consts], writes=[b_pb[pba_i]])
        kb.op("pe", lambda e: e.matmul(pb[pbb_i], onerow[:], rst_row[0:1, :],
                                       start=True, stop=True),
              reads=[b_rst, b_consts], writes=[b_pb[pbb_i]])

    def r_chunk_bf(g):
        def get(c):
            bi = c % 2
            kb.op("act", lambda e, c=c, bi=bi: e.activation(
                rb2[bi][:], r_t[:, c, g * G:(g + 1) * G], AF.Copy),
                reads=[b_r], writes=[b_rb[bi]])
            return [b_rb[bi]], rb2[bi][:]
        return get

    def ffn(s, upto="full"):
        ENTRY = {n: kb.cnt[n] for n in ("pe", "act", "dve", "pool")}
        kb.barrier()
        _rs = nc.named_scope("lnrt")
        _rs.__enter__()
        for g in range(NGRP):
            ln_stats(g, r_chunk_bf(g), 0, 1, 6, 7)
            for c in range(C):
                kb.op("dve", lambda e, c=c, g=g: e.tensor_add(
                    tln[:], r_t[:, c, g * G:(g + 1) * G], pb[6]),
                    reads=[b_r, b_pb[6]], writes=[b_tln])
                kb.op("dve", lambda e, c=c, g=g: e.tensor_mul(
                    a32[:, c, g * G:(g + 1) * G], tln[:], pb[7]),
                    reads=[b_tln, b_pb[7]], writes=[b_a32])
        if upto == "ln":
            for c in range(C):
                for g in range(NGRP):
                    kb.op("dve", lambda e, c=c, g=g: e.tensor_copy(
                        r_t[:, c, g * G:(g + 1) * G], a32[:, c, g * G:(g + 1) * G]),
                        reads=[b_a32], writes=[b_r])
            kb.barrier()
            return
        kb.op("pool", lambda e: e.memset(coeff[:], 1.0), writes=[b_coeff])
        for g in range(NGRP):
            def logits(e, g=g):
                for c in range(C):
                    ins = e.matmul(pb[0][0:16, :], rw_sb[:, c, :],
                                   a32[:, c, g * G:(g + 1) * G],
                                   start=(c == 0), stop=(c == C - 1))
                return ins
            kb.op("pe", logits, reads=[b_a32, b_consts], writes=[b_pb[0]])
            kb.op("dve", lambda e: e.tensor_copy(lgs[:], pb[0][0:16, :]),
                  reads=[b_pb[0]], writes=[b_lgs])
            for tt in range(4):
                col = g * G + tt * 128
                kb.op("pe", lambda e, tt=tt: e.transpose(
                    pb[1][:, 0:16], lgs[0:16, tt * 128:(tt + 1) * 128], idf[0:16, 0:16]),
                    reads=[b_lgs, b_consts], writes=[b_pb[1]])
                kb.op("dve", lambda e: e.tensor_copy(lt_t[:], pb[1][:, 0:16]),
                      reads=[b_pb[1]], writes=[b_small])
                kb.op("dve", lambda e: e.reduce_max(nm0[:], lt_t[:], axis=AX.X, negate=True),
                      reads=[b_small], writes=[b_small])
                kb.op("act", lambda e: e.activation(ex_t[:], lt_t[:], AF.Exp, bias=nm0[:, :]),
                      reads=[b_small], writes=[b_small])
                kb.op("dve", lambda e: e.reduce_sum(ssum[:], ex_t[:], axis=AX.X),
                      reads=[b_small], writes=[b_small])
                kb.op("dve", lambda e: e.reciprocal(rs_t[:], ssum[:]),
                      reads=[b_small], writes=[b_small])
                kb.op("dve", lambda e: e.reduce_max(m1_t[:], ex_t[:], axis=AX.X),
                      reads=[b_small], writes=[b_small])
                kb.op("dve", lambda e: e.tensor_single_scalar(eq_t[:], ex_t[:], m1_t[:, :],
                                                              ALU.is_equal),
                      reads=[b_small], writes=[b_small])
                kb.op("dve", lambda e: e.scalar_tensor_tensor(eq_t[:], eq_t[:], -1e30,
                                                              ex_t[:], ALU.mult, ALU.add),
                      reads=[b_small], writes=[b_small])
                kb.op("dve", lambda e: e.reduce_max(m2_t[:], eq_t[:], axis=AX.X),
                      reads=[b_small], writes=[b_small])
                kb.op("dve", lambda e: e.tensor_single_scalar(sel_r[:], ex_t[:], m2_t[:, :],
                                                              ALU.is_ge),
                      reads=[b_small], writes=[b_small])
                kb.op("dve", lambda e: e.tensor_mul(ex_t[:], ex_t[:], sel_r[:]),
                      reads=[b_small], writes=[b_small])
                kb.op("dve", lambda e: e.tensor_single_scalar(ex_t[:], ex_t[:], rs_t[:, :],
                                                              ALU.mult),
                      reads=[b_small], writes=[b_small])
                kb.op("pe", lambda e: e.transpose(pb[2][0:16, 0:128], ex_t[:], idf[:]),
                      reads=[b_small, b_consts], writes=[b_pb[2]])
                kb.op("dve", lambda e, col=col: e.tensor_copy(
                    coeff[0:16, col:col + 128], pb[2][0:16, 0:128]),
                    reads=[b_pb[2]], writes=[b_coeff])
        if upto == "router":
            kb.op("dve", lambda e: e.tensor_copy(r_t[0:32, 0, :], coeff[:, :]),
                  reads=[b_coeff], writes=[b_r])
            kb.barrier()
            _rs.__exit__(None, None, None)
            return
        kb.barrier()
        _rs.__exit__(None, None, None)

        _ls = nc.named_scope("exploop")
        _ls.__enter__()
        ffn_loop_body(*ffn_wload(ENTRY))
        _ls.__exit__(None, None, None)
        if upto == "sgil":
            kb.op("dve", lambda e: e.tensor_copy(r_t[:, 0, 0:G], sgil[:, :]),
                  reads=[b_sgil], writes=[b_r])
            kb.op("dve", lambda e: e.tensor_copy(r_t[:, 4, 0:G], cb_s[:]),
                  reads=[b_cbs], writes=[b_r])
            kb.barrier()
            return
        if upto == "inter":
            for ei in range(2):
                for fc in range(FCH):
                    kb.op("dve", lambda e, ei=ei, fc=fc: e.tensor_copy(
                        r_t[:, ei * 4 + fc, 0:G], inter[:, ei, fc, :]),
                        reads=[b_inter], writes=[b_r])
            kb.barrier()


    def ffn_wload(ENTRY):
        P0 = kb.cnt["pe"]
        A0 = kb.cnt["act"]
        D0 = kb.cnt["dve"]
        WSL = (kb.cnt["wA"], kb.cnt["wB"], kb.cnt["wC"])
        WNM = ("wA", "wB", "wC")
        sems = kb.sem
        pe, act, dve, pool, sync = (kb.e["pe"], kb.e["act"], kb.e["dve"],
                                    kb.e["pool"], kb.e["sync"])

        def gu_src(e):
            if e < 16:
                return (gw_d[e].rearrange("(kc p) f -> p kc f", p=128),
                        uw_d[e].rearrange("(kc p) f -> p kc f", p=128))
            h0 = (e - 16) * 512
            return (sg_d[:, h0:h0 + 512].rearrange("(kc p) f -> p kc f", p=128),
                    su_d[:, h0:h0 + 512].rearrange("(kc p) f -> p kc f", p=128))

        def d_src(e):
            if e < 16:
                return dw_d[e].rearrange("(kc p) f -> p kc f", p=128)
            h0 = (e - 16) * 512
            return sd_d[h0:h0 + 512, :].rearrange("(kc p) f -> p kc f", p=128)

        NP = NE // 2
        for _sn in ("g0", "g1"):
            if kb.cnt[_sn] > 0 and kb.hwm["pool"][_sn] < kb.cnt[_sn]:
                pool.wait_ge(sems[_sn], kb.cnt[_sn])
                kb.hwm["pool"][_sn] = kb.cnt[_sn]
        for e_n, e_v in ENTRY.items():
            if e_n != "pool" and e_v > 0:
                pool.wait_ge(sems[e_n], e_v)
        for j in range(3 * NP):
            p3, r3 = divmod(j, 3)
            sl = slot[r3]
            if p3 >= 1:
                pm = p3 - 1
                if r3 == 0:
                    pool.wait_ge(sems["pe"], P0 + pm * 104 + 88)
                elif r3 == 1:
                    pool.wait_ge(sems["pe"], P0 + pm * 104 + 96)
                else:
                    pool.wait_ge(sems["pe"], P0 + pm * 104 + 104)
            wsem = WNM[r3]
            if r3 < 2:
                e_g = 2 * p3 + r3
                sga, sua = gu_src(e_g)
                pool.dma_start(out=sl[:, 0:4096].rearrange("p (kc f) -> p kc f", kc=8),
                               in_=sga).then_inc(sems[wsem], 16)
                pool.dma_start(out=sl[:, 4096:8192].rearrange("p (kc f) -> p kc f", kc=8),
                               in_=sua).then_inc(sems[wsem], 16)
                if r3 == 0:
                    pool.dma_start(out=sel_fx[:, :],
                                   in_=c_sel[:, 2 * p3 * 128:(2 * p3 + 2) * 128]
                                   ).then_inc(sems[wsem], 16)
                    kb.cnt[wsem] += 16
            else:
                pool.dma_start(out=sl[:, 0:4096].rearrange("p (kc f) -> p kc f", kc=4),
                               in_=d_src(2 * p3)).then_inc(sems[wsem], 16)
                pool.dma_start(out=sl[:, 4096:8192].rearrange("p (kc f) -> p kc f", kc=4),
                               in_=d_src(2 * p3 + 1)).then_inc(sems[wsem], 16)
            kb.cnt[wsem] += 32

        return P0, A0, D0, WSL, WNM

    def ffn_loop_body(P0, A0, D0, WSL, WNM):
        sems = kb.sem
        pe, act, dve = kb.e["pe"], kb.e["act"], kb.e["dve"]
        NP = NE // 2
        for p in range(NP):
            for g in range(NGRP):
                pg = p * 4 + g
                base = P0 + p * 104 + g * 26
                Dg = D0 + pg * 26
                for ei in range(2):
                    mm = pg * 2 + ei
                    st = base + ei * 9
                    sl = slot[ei]
                    if ei == 1:
                        pe.wait_ge(sems["dve"], Dg + 1)
                    elif pg >= 1:
                        pe.wait_ge(sems["dve"], Dg - 26 + 18 + 4 + 1)
                    if g == 0:
                        mult = 48 if ei == 0 else 32
                        pe.wait_ge(sems[WNM[ei]], WSL[ei] + mult * (p + 1))
                        if ei == 1:
                            pe.wait_ge(sems[WNM[0]], WSL[0] + 48 * (p + 1))
                    pe.matmul(pb[4], sel_fx[:, ei * 128:(ei + 1) * 128],
                              coeff[:, g * G:(g + 1) * G], start=True, stop=True
                              ).then_inc(sems["pe"], 1)
                    for fc in range(FCH):
                        if ei == 1:
                            pe.wait_ge(sems["dve"], Dg + 3 + 2 * fc)
                        elif pg >= 1:
                            pe.wait_ge(sems["dve"], Dg - 26 + 18 + fc + 1)
                        for kc in range(C):
                            ins = pe.matmul(
                                pb[fc],
                                sl[:, kc * 512 + fc * 128:kc * 512 + (fc + 1) * 128],
                                a32[:, kc, g * G:(g + 1) * G],
                                start=(kc == 0), stop=(kc == C - 1))
                        ins.then_inc(sems["pe"], 1)
                    for fc in range(FCH):
                        pe.wait_ge(sems["act"], A0 + mm * 4 + fc + 1)
                        for kc in range(C):
                            ins = pe.matmul(
                                pb[fc],
                                sl[:, 4096 + kc * 512 + fc * 128:
                                   4096 + kc * 512 + (fc + 1) * 128],
                                a32[:, kc, g * G:(g + 1) * G],
                                start=(kc == 0), stop=(kc == C - 1))
                        ins.then_inc(sems["pe"], 1)
                pe.wait_ge(sems["dve"], Dg + 18)
                if g == 0:
                    pe.wait_ge(sems[WNM[2]], WSL[2] + 32 * (p + 1))
                dsl = slot[2]
                for fo in range(8):
                    if fo >= 5 and pg >= 1:
                        pe.wait_ge(sems["dve"], Dg - 26 + 18 + fo + 1)
                    first = True
                    for ei in range(2):
                        for kc in range(FCH):
                            ins = pe.matmul(
                                pb[fo],
                                dsl[:, ei * 4096 + kc * 1024 + fo * 128:
                                    ei * 4096 + kc * 1024 + (fo + 1) * 128],
                                inter[:, ei, kc, :],
                                start=first, stop=(ei == 1 and kc == FCH - 1))
                            first = False
                    ins.then_inc(sems["pe"], 1)
        kb.cnt["pe"] += NP * 104

        for mm in range(8 * NP):
            pg, ei = divmod(mm, 2)
            p, g = divmod(pg, 4)
            st = P0 + p * 104 + g * 26 + ei * 9
            for fc in range(FCH):
                act.wait_ge(sems["pe"], st + 2 + fc)
                if fc >= 1:
                    act.wait_ge(sems["dve"], D0 + pg * 26 + ei * 9 + 1 + 2 * fc)
                elif mm >= 1:
                    pgm, eim = divmod(mm - 1, 2)
                    act.wait_ge(sems["dve"], D0 + pgm * 26 + eim * 9 + 9)
                act.activation(sgil[:, :], pb[fc], AF.Silu).then_inc(sems["act"], 1)
        kb.cnt["act"] += 8 * NP * 4

        dcnt = D0
        for p in range(NP):
            for g in range(NGRP):
                pg = p * 4 + g
                for ei in range(2):
                    mm = pg * 2 + ei
                    st = P0 + p * 104 + g * 26 + ei * 9
                    if ei == 0:
                        dve.wait_ge(sems["pe"], P0 + p * 104 + g * 26)
                    dve.wait_ge(sems["pe"], st + 1)
                    if mm >= 1:
                        dve.wait_ge(sems["dve"], dcnt)
                    dve.tensor_copy(cb_s[:], pb[4]).then_inc(sems["dve"], 1)
                    dcnt += 1
                    for fc in range(FCH):
                        dve.wait_ge(sems["act"], A0 + mm * 4 + fc + 1)
                        dve.wait_ge(sems["dve"], dcnt)
                        dve.tensor_mul(sgil[:, :], sgil[:, :],
                                       cb_s[:]).then_inc(sems["dve"], 1)
                        dcnt += 1
                        dve.wait_ge(sems["pe"], st + 6 + fc)
                        dve.wait_ge(sems["dve"], dcnt)
                        dve.tensor_mul(inter[:, ei, fc, :],
                                       sgil[:, :], pb[fc]).then_inc(sems["dve"], 1)
                        dcnt += 1
                for fo in range(8):
                    dve.wait_ge(sems["pe"], P0 + p * 104 + g * 26 + 18 + fo + 1)
                    dve.scalar_tensor_tensor(r_t[:, fo, g * G:(g + 1) * G],
                                             pb[fo], 0.5,
                                             r_t[:, fo, g * G:(g + 1) * G],
                                             ALU.mult, ALU.add).then_inc(sems["dve"], 1)
                    dcnt += 1
        kb.cnt["dve"] += NP * 104
        kb.barrier()
    def attn(s):
        kb.barrier()
        kb.sync_join()
        kb.op("pool", lambda e: e.memset(qm2[:], 0.0), writes=[b_qm])
        kb.op("pool", lambda e: e.memset(km2[:], 0.0), writes=[b_km])
        kb.dma("sync", cosS[:], c_cos[:, :], writes=[b_cos])
        kb.dma("sync", sinS[:], c_sin[:, :], writes=[b_sin])

        def yb_chunk(c):
            return [b_yb], yb[:, c, :]

        _qs = nc.named_scope("qkv")
        _qs.__enter__()
        for g in range(NGRP):
            gsl = slice(g * G, (g + 1) * G)
            ln_stats(g, r_chunk_bf(g), 0, 1, 6, 7)
            for c in range(C):
                kb.op("dve", lambda e, c=c, gsl=gsl: e.tensor_add(
                    tln[:], r_t[:, c, gsl], pb[6]),
                    reads=[b_r, b_pb[6]], writes=[b_tln])
                kb.op("dve", lambda e, c=c: e.tensor_mul(yb[:, c, :], tln[:], pb[7]),
                      reads=[b_tln, b_pb[7]], writes=[b_yb])
            ln_stats(g, yb_chunk, 0, 1, 4, 5)
            for c in range(C):
                kb.op("dve", lambda e, c=c: e.tensor_add(tln[:], yb[:, c, :], pb[4]),
                      reads=[b_yb, b_pb[4]], writes=[b_tln])
                kb.op("dve", lambda e, c=c: e.tensor_mul(ys_q[:], tln[:], pb[5][:, :]),
                      reads=[b_tln, b_pb[5]], writes=[b_ys])
                kb.op("dve", lambda e, c=c: e.tensor_scalar(
                    xn3[:, c, :], ys_q[:], gam_t[:, c:c + 1], bet_t[:, c:c + 1],
                    ALU.mult, ALU.add),
                    reads=[b_ys, b_bias], writes=[b_xn3])
            for fo in range(24):
                wbuf = fo % 2
                kb.dma("pool", wi_b[wbuf][:],
                       Wi_d[:, fo * 128:(fo + 1) * 128].rearrange("(c p) f -> p c f", p=128),
                       writes=[b_wi[wbuf]])
                pj = fo % 3
                def qkvmm(e, wbuf=wbuf, pj=pj):
                    for c in range(C):
                        ins = e.matmul(pb[pj], wi_b[wbuf][:, c, :], xn3[:, c, :],
                                       start=(c == 0), stop=(c == C - 1))
                    return ins
                kb.op("pe", qkvmm, reads=[b_wi[wbuf], b_xn3], writes=[b_pb[pj]])
                if fo < 16:
                    isq = fo < 8
                    dst = q_rope if isq else a32
                    dstb = b_qrope if isq else b_a32
                    cc = fo if isq else fo - 8
                    kb.op("dve", lambda e, pj=pj, fo=fo: e.tensor_single_scalar(
                        qpre[:], pb[pj], bias_in[:, fo:fo + 1], ALU.add),
                        reads=[b_pb[pj], b_bias], writes=[b_qpre])
                    kb.op("pe", lambda e: e.matmul(pb[3][:, :], pswap[:], qpre[:],
                                                   start=True, stop=True),
                          reads=[b_qpre, b_consts], writes=[b_pb[3]])
                    kb.op("dve", lambda e, gsl=gsl: e.tensor_mul(tln[:], qpre[:],
                                                                 cosS[:, gsl]),
                          reads=[b_qpre, b_cos], writes=[b_tln])
                    kb.op("dve", lambda e, gsl=gsl: e.tensor_mul(ys_q[:], pb[3][:, :],
                                                                 sinS[:, gsl]),
                          reads=[b_pb[3], b_sin], writes=[b_ys])
                    kb.op("dve", lambda e, dst=dst, cc=cc, gsl=gsl: e.tensor_add(
                        dst[:, cc, gsl], tln[:], ys_q[:]),
                        reads=[b_tln, b_ys], writes=[dstb])
                    kb.op("act", lambda e, dst=dst, cc=cc, gsl=gsl: e.activation(
                        sqb[:], dst[:, cc, gsl], AF.Square),
                        reads=[dstb], writes=[b_sqb])
                    kb.op("pe", lambda e: e.matmul(pb[4][0:2, :], hd2_t[:], sqb[:],
                                                   start=True, stop=True),
                          reads=[b_sqb, b_consts], writes=[b_pb[4]])
                    kb.op("dve", lambda e: e.reduce_max(rm2[:], pb[4][0:2, :], axis=AX.X),
                          reads=[b_pb[4]], writes=[b_small])
                    mt = qm2 if isq else km2
                    mb = b_qm if isq else b_km
                    kb.op("dve", lambda e, mt=mt, cc=cc: e.tensor_max(
                        mt[0:2, cc:cc + 1], mt[0:2, cc:cc + 1], rm2[:]),
                        reads=[b_small], writes=[mb])
                else:
                    vc = fo - 16
                    kb.op("dve", lambda e, pj=pj, fo=fo: e.tensor_single_scalar(
                        qpre[:], pb[pj], bias_in[:, fo:fo + 1], ALU.add),
                        reads=[b_pb[pj], b_bias], writes=[b_qpre])
                    for t4 in range(4):
                        kb.op("pe", lambda e, t4=t4: e.transpose(
                            pb[5].bitcast(BF16)[:, 0:128],
                            qpre[:, t4 * 128:(t4 + 1) * 128], idb[:]),
                            reads=[b_qpre, b_consts], writes=[b_pb[5]])
                        kt = g * 4 + t4
                        hA = 2 * vc
                        kb.op("dve", lambda e, kt=kt, hA=hA: e.tensor_copy(
                            vaug[:, kt, hA * 65:hA * 65 + 64],
                            pb[5].bitcast(BF16)[:, 0:64]),
                            reads=[b_pb[5]], writes=[b_vaug])
                        kb.op("dve", lambda e, kt=kt, hA=hA: e.tensor_copy(
                            vaug[:, kt, (hA + 1) * 65:(hA + 1) * 65 + 64],
                            pb[5].bitcast(BF16)[:, 64:128]),
                            reads=[b_pb[5]], writes=[b_vaug])

        kb.op("dve", lambda e: e.tensor_mul(nsh[:], qm2[:], km2[:]),
              reads=[b_qm, b_km], writes=[b_sh])
        kb.op("act", lambda e: e.activation(nsh2[:], nsh[:], AF.Sqrt),
              reads=[b_sh], writes=[b_sh])
        kb.op("dve", lambda e: e.tensor_single_scalar(nsh[:], nsh2[:], -0.125, ALU.mult),
              reads=[b_sh], writes=[b_sh])
        for rr in range(2):
            kb.op("pe", lambda e, rr=rr: e.matmul(
                pb[6][:, rr * 8:(rr + 1) * 8], bas2_t[:, rr * 128:(rr + 1) * 128],
                nsh[0:2, 0:8], start=True, stop=True),
                reads=[b_sh, b_consts], writes=[b_pb[6]])
        kb.op("dve", lambda e: e.tensor_copy(shbc[:], pb[6][:, 0:16]),
              reads=[b_pb[6]], writes=[b_sh])
        _qs.__exit__(None, None, None)
        kb.barrier()
        kb.sync_join()
        _as = nc.named_scope("aloop")
        _as.__enter__()

        for qg in range(NGRP):
            qsl = slice(qg * G, (qg + 1) * G)
            for h in range(H):
                c2 = h // 2
                ro = (h % 2) * 64
                jcol = (h % 2) * 8 + c2
                pc = 4 + (h % 2)
                for kt in range(NKT):
                    pj = kt % 3
                    kb.op("pe", lambda e, pj=pj, c2=c2, ro=ro, kt=kt, qsl=qsl: e.matmul(
                        pb[pj],
                        a32[ro:ro + 64, c2, kt * 128:(kt + 1) * 128],
                        q_rope[ro:ro + 64, c2, qsl], start=True, stop=True),
                        reads=[b_a32, b_qrope], writes=[b_pb[pj]])
                    kb.op("act", lambda e, pj=pj, kt=kt, jcol=jcol: e.activation(
                        ptil[:, kt, :], pb[pj], AF.Exp,
                        bias=shbc[:, jcol:jcol + 1], scale=0.125),
                        reads=[b_pb[pj], b_sh], writes=[b_pt])
                def ctxmm(e, pc=pc, h=h):
                    for kt in range(NKT):
                        ins = e.matmul(pb[pc][0:65, :], vaug[:, kt, h * 65:h * 65 + 65],
                                       ptil[:, kt, :], start=(kt == 0), stop=(kt == NKT - 1))
                    return ins
                kb.op("pe", ctxmm, reads=[b_pt, b_vaug], writes=[b_pb[pc]])
                kb.op("dve", lambda e, pc=pc: e.reciprocal(recl_row[:], pb[pc][64:65, :]),
                      reads=[b_pb[pc]], writes=[b_recl])
                kb.op("pe", lambda e: e.matmul(pb[7][0:64, :], onerow[0:1, 0:64],
                                               recl_row[0:1, :], start=True, stop=True),
                      reads=[b_recl, b_consts], writes=[b_pb[7]])
                kb.op("act", lambda e, pc=pc: e.activation(cf_t[0:64, :], pb[pc][0:64, :],
                                                           AF.Copy),
                      reads=[b_pb[pc]], writes=[b_cf])
                kb.op("dve", lambda e, c2=c2, ro=ro: e.tensor_mul(
                    ctxf[ro:ro + 64, c2, :], cf_t[0:64, :], pb[7][0:64, :]),
                    reads=[b_cf, b_pb[7]], writes=[b_ctxf])
            ln_stats(qg, r_chunk_bf(qg), 0, 1, 2, 3)
            for fo in range(C):
                wbuf = fo % 2
                kb.dma("pool", wo_b[wbuf][:],
                       Wo_d[:, fo * 128:(fo + 1) * 128].rearrange("(c p) f -> p c f", p=128),
                       writes=[b_wo[wbuf]])
                pj = fo % 2 + 6
                def projmm(e, wbuf=wbuf, pj=pj):
                    for c in range(C):
                        ins = e.matmul(pb[pj], wo_b[wbuf][:, c, :], ctxf[:, c, :],
                                       start=(c == 0), stop=(c == C - 1))
                    return ins
                kb.op("pe", projmm, reads=[b_wo[wbuf], b_ctxf], writes=[b_pb[pj]])
                kb.op("dve", lambda e, fo=fo, qsl=qsl: e.tensor_add(
                    tln[:], r_t[:, fo, qsl], pb[2][:, :]),
                    reads=[b_r, b_pb[2]], writes=[b_tln])
                kb.op("dve", lambda e: e.tensor_mul(ys_l[:], tln[:], pb[3][:, :]),
                      reads=[b_tln, b_pb[3]], writes=[b_cf])
                kb.op("dve", lambda e, pj=pj, fo=fo: e.scalar_tensor_tensor(
                    tln[:], pb[pj], bias_out[:, fo:fo + 1], ys_l[:], ALU.add, ALU.add),
                    reads=[b_pb[pj], b_cf, b_bias], writes=[b_tln])
                kb.op("dve", lambda e, fo=fo, qsl=qsl: e.tensor_add(
                    r_t[:, fo, qsl], r_t[:, fo, qsl], tln[:]),
                    reads=[b_tln], writes=[b_r])
        _as.__exit__(None, None, None)
        kb.barrier()

    def store(s):
        for tt in range(NKT):
            bi = tt % 2
            for c in range(C):
                j = (tt * C + c) % 6
                kb.op("pe", lambda e, c=c, tt=tt, j=j: e.transpose(
                    pb[j][:, 0:128], r_t[:, c, tt * 128:(tt + 1) * 128], idf[:]),
                    reads=[b_r, b_consts], writes=[b_pb[j]])
                kb.op("dve", lambda e, c=c, bi=bi, j=j: e.tensor_copy(
                    stg[bi][:, c * 128:(c + 1) * 128], pb[j][:, 0:128]),
                    reads=[b_pb[j]], writes=[b_stg[bi]])
            kb.dma("sync", OUT[s, tt * 128:(tt + 1) * 128, :], stg[bi][:],
                   reads=[b_stg[bi]])

    from contextlib import nullcontext
    def scope(name):
        return nc.named_scope(name)
    with scope("ph0"):
        ph0()
    for s in range(nseq):
        kb.barrier()
        if stage == "wc":
            wctap()
            store(s)
            continue
        with scope(f"load{s}"):
            load(s)
        if stage != "load":
            with scope(f"ffnA{s}"):
                ffn(s, upto=(stage if stage in ("ln", "router", "inter", "sgil") else "full"))
        if stage in ("attn", "full"):
            with scope(f"attn{s}"):
                attn(s)
        if stage == "full":
            with scope(f"ffnB{s}"):
                ffn(s)
        kb.barrier()
        with scope(f"store{s}"):
            store(s)
    for _sn in ("d0", "d1", "d2", "d3", "d4", "d5", "g0", "g1",
                "wA", "wB", "wC"):
        if kb.cnt[_sn] > 0:
            kb.e["sync"].wait_ge(kb.sem[_sn], kb.cnt[_sn])
    return nc


WNAMES = ["L0_attn_W_in", "L0_attn_b_in", "L0_attn_W_out", "L0_attn_b_out",
          "L0_attn_gamma", "L0_attn_beta", "L0_ffn_router_w", "L0_ffn_gate_w",
          "L0_ffn_up_w", "L0_ffn_down_w", "L0_ffn_shared_gate_w",
          "L0_ffn_shared_up_w", "L0_ffn_shared_down_w"]


def kernel(**inputs):
    X = np.ascontiguousarray(np.asarray(inputs["L0_attn_X"], dtype=np.float32))
    weights = {n: np.ascontiguousarray(np.asarray(inputs[n], dtype=np.float32))
               for n in WNAMES}
    shards = np.split(X, 8, axis=0)
    nc = build_nc(nseq=4, stage="full")
    in_maps = [dict(weights, L0_attn_X=shards[i]) for i in range(8)]
    res = run_bass_kernel_spmd(nc, in_maps, core_ids=list(range(8)))
    return np.concatenate([r["OUT"] for r in res.results], axis=0)


# version 08
# speedup vs baseline: 1.3674x; 1.0646x over previous
import numpy as np
import ml_dtypes
import concourse.bass as bass
import concourse.mybir as mybir
from concourse.bass_utils import run_bass_kernel_spmd

F32 = mybir.dt.float32
BF16 = mybir.dt.bfloat16
AF = mybir.ActivationFunctionType
ALU = mybir.AluOpType
AX = mybir.AxisListType
bfdt = ml_dtypes.bfloat16

PH0_PARTS = ("consts", "vaug", "bias", "slabs", "rw")

S = 2048
G = 512
NGRP = 4
D = 1024
C = 8
NE = 18
F = 512
FCH = 4
H = 16
DH = 64
NKT = 16
EPS = 1e-5
KB = 1024


class Buf:
    __slots__ = ("w", "r", "name")

    def __init__(self, name):
        self.name = name
        self.w = None
        self.r = {}


class Kb:
    ENGS = ("pe", "act", "dve", "pool", "sync")

    def __init__(self, nc):
        self.nc = nc
        self.e = {"pe": nc.tensor, "act": nc.scalar, "dve": nc.vector,
                  "pool": nc.gpsimd, "sync": nc.sync}
        names = (*self.ENGS, "dmaS", "dmaG", "wA", "wB", "wC",
                 "d0", "d1", "d2", "d3", "d4", "d5", "g0", "g1")
        self.sem = {n: nc.alloc_semaphore("s_" + n) for n in names}
        self.cnt = {n: 0 for n in names}
        self.hwm = {n: {m: 0 for m in names} for n in self.ENGS}
        self.bufs = []
        self.rot = {"sync": ["d0", "d1", "d2", "d3", "d4", "d5"],
                    "pool": ["g0", "g1"]}
        self.rix = {"sync": 0, "pool": 0}

    def buf(self, name):
        b = Buf(name)
        self.bufs.append(b)
        return b

    def _waits(self, eng, deps):
        best = {}
        for d in deps:
            if d is None:
                continue
            s, v = d
            if v and v > best.get(s, 0):
                best[s] = v
        for s, v in best.items():
            if self.hwm[eng][s] < v:
                self.e[eng].wait_ge(self.sem[s], v)
                self.hwm[eng][s] = v

    def op(self, eng, fn, reads=(), writes=(), extra=()):
        deps = list(extra)
        for b in reads:
            deps.append(b.w)
        for b in writes:
            deps.append(b.w)
            deps.extend(b.r.items())
        self._waits(eng, deps)
        ins = fn(self.e[eng])
        ins.then_inc(self.sem[eng], 1)
        self.cnt[eng] += 1
        v = self.cnt[eng]
        for b in reads:
            b.r[eng] = v
        for b in writes:
            b.w = (eng, v)
            b.r = {}
        return v

    def dma(self, q, out, in_, reads=(), writes=(), extra=(), **kw):
        eng = "sync" if q == "sync" else "pool"
        deps = list(extra)
        for b in reads:
            deps.append(b.w)
        for b in writes:
            deps.append(b.w)
            deps.extend(b.r.items())
        self._waits(eng, deps)
        sems = self.rot[eng]
        dsem = sems[self.rix[eng] % len(sems)]
        self.rix[eng] += 1
        if self.cnt[dsem] > 0:
            self._waits(eng, [(dsem, self.cnt[dsem])])
        self.e[eng].dma_start(out=out, in_=in_, **kw).then_inc(self.sem[dsem], 16)
        self.cnt[dsem] += 16
        v = self.cnt[dsem]
        for b in reads:
            b.r[dsem] = v
        for b in writes:
            b.w = (dsem, v)
            b.r = {}
        return v

    BAR_ENGS = ("pe", "act", "dve", "pool")

    def sync_join(self):
        for n in self.BAR_ENGS:
            if self.cnt[n] > 0:
                self.e["sync"].wait_ge(self.sem[n], self.cnt[n])
                self.hwm["sync"][n] = self.cnt[n]

    def barrier(self):
        import concourse.mybir as _mb
        self.nc.multi_engine_barrier([_mb.EngineType.PE, _mb.EngineType.Activation,
                                      _mb.EngineType.DVE, _mb.EngineType.Pool])
        for n in self.BAR_ENGS:
            for m in self.BAR_ENGS:
                self.hwm[n][m] = self.cnt[m]


def _rope_tables():
    inv = 1.0 / (10000.0 ** (np.arange(0, 32, 2, dtype=np.float64) / 32.0))
    t = np.arange(S, dtype=np.float64)
    fr = t[:, None] * inv[None, :]
    fr = np.repeat(fr, 2, axis=1)
    cos = np.ones((128, S), dtype=np.float64)
    sin = np.zeros((128, S), dtype=np.float64)
    for base in (0, 64):
        cos[base:base + 32, :] = np.cos(fr).T
        sin[base:base + 32, :] = np.sin(fr).T
    return cos.astype(bfdt), sin.astype(bfdt)


def _pswap():
    p = np.zeros((128, 128), dtype=np.float32)
    for base in (0, 64):
        for i in range(16):
            p[base + 2 * i + 1, base + 2 * i] = -1.0
            p[base + 2 * i, base + 2 * i + 1] = 1.0
    return p.astype(bfdt)


def build_nc(nseq=4, stage="full"):
    nc = bass.Bass("TRN2", target_bir_lowering=False, dynamic_dma_scratch_size=16384)

    Xd = nc.dram_tensor("L0_attn_X", [nseq, S, D], F32, kind="ExternalInput").ap()
    Wi_d = nc.dram_tensor("L0_attn_W_in", [D, 3 * D], F32, kind="ExternalInput").ap()
    bi_d = nc.dram_tensor("L0_attn_b_in", [3 * D], F32, kind="ExternalInput").ap()
    Wo_d = nc.dram_tensor("L0_attn_W_out", [D, D], F32, kind="ExternalInput").ap()
    bo_d = nc.dram_tensor("L0_attn_b_out", [D], F32, kind="ExternalInput").ap()
    ga_d = nc.dram_tensor("L0_attn_gamma", [D], F32, kind="ExternalInput").ap()
    be_d = nc.dram_tensor("L0_attn_beta", [D], F32, kind="ExternalInput").ap()
    rw_d = nc.dram_tensor("L0_ffn_router_w", [D, 16], F32, kind="ExternalInput").ap()
    gw_d = nc.dram_tensor("L0_ffn_gate_w", [16, D, F], F32, kind="ExternalInput").ap()
    uw_d = nc.dram_tensor("L0_ffn_up_w", [16, D, F], F32, kind="ExternalInput").ap()
    dw_d = nc.dram_tensor("L0_ffn_down_w", [16, F, D], F32, kind="ExternalInput").ap()
    sg_d = nc.dram_tensor("L0_ffn_shared_gate_w", [D, D], F32, kind="ExternalInput").ap()
    su_d = nc.dram_tensor("L0_ffn_shared_up_w", [D, D], F32, kind="ExternalInput").ap()
    sd_d = nc.dram_tensor("L0_ffn_shared_down_w", [D, D], F32, kind="ExternalInput").ap()
    OUT = nc.dram_tensor("OUT", [nseq, S, D], F32, kind="ExternalOutput").ap()


    cosT, sinT = _rope_tables()
    c_cos = nc.inline_tensor(cosT, name="c_cos").ap()
    c_sin = nc.inline_tensor(sinT, name="c_sin").ap()
    c_pswap = nc.inline_tensor(_pswap(), name="c_pswap").ap()
    c_idf = nc.inline_tensor(np.eye(128, dtype=np.float32), name="c_idf").ap()
    c_idb = nc.inline_tensor(np.eye(128, dtype=bfdt), name="c_idb").ap()
    c_ones_col = nc.inline_tensor(np.ones((128, 1), dtype=bfdt), name="c_ones_col").ap()
    c_ones_row = nc.inline_tensor(np.ones((1, 128), dtype=np.float32), name="c_ones_row").ap()
    c_ones64 = nc.inline_tensor(np.ones((1, 64), dtype=np.float32), name="c_ones64").ap()
    hd2 = np.zeros((128, 2), dtype=bfdt)
    hd2[0:64, 0] = 1.0
    hd2[64:128, 1] = 1.0
    c_hd2 = nc.inline_tensor(hd2, name="c_hd2").ap()
    sel = np.zeros((32, NE * 128), dtype=bfdt)
    for e in range(NE):
        sel[e, e * 128:(e + 1) * 128] = 1.0
    c_sel = nc.inline_tensor(sel, name="c_sel").ap()
    b2 = np.zeros((2, 256), dtype=np.float32)
    b2[0, 0:128] = 1.0
    b2[1, 128:256] = 1.0
    c_basis2 = nc.inline_tensor(b2, name="c_basis2").ap()

    r_t = nc.alloc_sbuf_tensor("r_t", [128, C, S], F32)
    a32 = nc.alloc_sbuf_tensor("a32", [128, C, S], BF16)
    vaug = nc.alloc_sbuf_tensor("vaug", [128, NKT, H * 65], BF16)
    rb2 = [nc.alloc_sbuf_tensor(f"rb{i}", [128, G], BF16) for i in range(2)]
    qb2 = [nc.alloc_sbuf_tensor("qb0", [128, G], BF16)]
    qb2 = [qb2[0], qb2[0]]
    tln = nc.alloc_sbuf_tensor("tln", [128, G], F32)
    nmu_row = nc.alloc_sbuf_tensor("nmu_row", [1, G], F32)
    rst_row = nc.alloc_sbuf_tensor("rst_row", [1, G], F32)
    scr_row = nc.alloc_sbuf_tensor("scr_row", [1, G], F32)
    bias_in = nc.alloc_sbuf_tensor("bias_in", [128, 24], F32)
    bias_out = nc.alloc_sbuf_tensor("bias_out", [128, C], F32)
    gam_t = nc.alloc_sbuf_tensor("gam_t", [128, C], F32)
    bet_t = nc.alloc_sbuf_tensor("bet_t", [128, C], F32)
    rw_sb = nc.alloc_sbuf_tensor("rw_sb", [128, C, 16], BF16)
    sel_fx = nc.alloc_sbuf_tensor("sel_fx", [32, 256], BF16)
    idf = nc.alloc_sbuf_tensor("idf", [128, 128], F32)
    idb = nc.alloc_sbuf_tensor("idb", [128, 128], BF16)
    pswap = nc.alloc_sbuf_tensor("pswap", [128, 128], BF16)
    onecol = nc.alloc_sbuf_tensor("onecol", [128, 1], BF16)
    onerow = nc.alloc_sbuf_tensor("onerow", [1, 128], F32)
    one64 = onerow
    hd2_t = nc.alloc_sbuf_tensor("hd2_t", [128, 2], BF16)
    bas2_t = nc.alloc_sbuf_tensor("bas2_t", [2, 256], F32)
    epsb = nc.alloc_sbuf_tensor("epsb", [128, 1], F32)
    qm2 = nc.alloc_sbuf_tensor("qm2", [2, 8], F32)
    km2 = nc.alloc_sbuf_tensor("km2", [2, 8], F32)
    nsh = nc.alloc_sbuf_tensor("nsh", [2, 8], F32)
    nsh2 = nc.alloc_sbuf_tensor("nsh2", [2, 8], F32)
    shbc = nc.alloc_sbuf_tensor("shbc", [128, 16], F32)
    lt_t = nc.alloc_sbuf_tensor("lt_t", [128, 16], F32)
    ex_t = nc.alloc_sbuf_tensor("ex_t", [128, 16], F32)
    eq_t = nc.alloc_sbuf_tensor("eq_t", [128, 16], F32)
    sel_r = nc.alloc_sbuf_tensor("sel_r", [128, 16], F32)
    nm0 = nc.alloc_sbuf_tensor("nm0", [128, 1], F32)
    ssum = nc.alloc_sbuf_tensor("ssum", [128, 1], F32)
    rs_t = nc.alloc_sbuf_tensor("rs_t", [128, 1], F32)
    m1_t = nc.alloc_sbuf_tensor("m1_t", [128, 1], F32)
    m2_t = nc.alloc_sbuf_tensor("m2_t", [128, 1], F32)
    rm2 = nc.alloc_sbuf_tensor("rm2", [2, 1], F32)

    AR = 64 * KB
    arena = nc.alloc_sbuf_tensor("arena", [128, AR], mybir.dt.uint8)
    abase = nc.lookup_mloc(arena).addr

    def at(name, shape, dtype, off):
        return nc.alloc_sbuf_tensor_at(name, shape, dtype, offset=abase + off)

    slot = [at(f"slot{i}", [128, 8192], BF16, i * 16 * KB) for i in range(3)]
    inter = at("inter", [128, 2, FCH, G], BF16, 48 * KB)
    cb_s = at("cb_s", [128, G], F32, 56 * KB)
    lgs = at("lgs", [16, G], F32, 56 * KB)
    sgil = at("sgil", [128, G], F32, 58 * KB)
    coeff = at("coeff", [32, S], BF16, 60 * KB)
    q_rope = at("q_rope", [128, C, S], BF16, 0)
    wi_b = [at(f"wi{i}", [128, C, 128], BF16, 32 * KB + i * 2 * KB) for i in range(2)]
    cosS = at("cosS", [128, S], BF16, 36 * KB)
    sinS = at("sinS", [128, S], BF16, 40 * KB)
    yb = at("yb", [128, C, G], BF16, 44 * KB)
    xn3 = at("xn3", [128, C, G], BF16, 52 * KB)
    qpre = at("qpre", [128, G], BF16, 60 * KB)
    sqb = at("sqb", [128, G], BF16, 61 * KB)
    ys_q = at("ys_q", [128, G], F32, 62 * KB)
    cf_t = at("cf_t", [128, G], F32, 34 * KB)
    ys_l = at("ys_l", [128, G], F32, 34 * KB)
    ptil = at("ptil", [128, NKT, G], BF16, 36 * KB)
    ctxf = at("ctxf", [128, C, G], BF16, 52 * KB)
    wo_b = [at(f"wo{i}", [128, C, 128], BF16, 60 * KB + i * 2 * KB) for i in range(2)]
    stg = [at(f"stg{i}", [128, D], F32, i * 4 * KB) for i in range(2)]
    rowtmp = at("rowtmp", [1, 128], F32, 48 * KB)
    recl_row = at("recl_row", [1, G], F32, 32 * KB)
    p0a = [at(f"p0a{i}", [128, 4096], F32, i * 16 * KB) for i in range(2)]
    p0b = [at(f"p0b{i}", [128, 4096], BF16, 32 * KB + i * 8 * KB) for i in range(2)]

    pball = nc.alloc_psum_tensor("pball", [128, 8, 512], F32)
    pb = [pball[:, i, :] for i in range(8)]

    kb = Kb(nc)
    B = kb.buf
    b_r = B("r")
    b_a32 = B("a32")
    b_vaug = B("vaug")
    b_inter = B("inter")
    b_sgil = B("sgil")
    b_cbs = B("cb_s")
    b_coeff = B("coeff")
    b_rb = [B("rb0"), B("rb1")]
    _bq = B("qb0")
    b_qb = [_bq, _bq]
    b_tln = B("tln")
    b_ys = B("ysq")
    b_ts2 = B("tln")

    b_nmu = B("nmu_row")
    b_rst = B("rst_row")
    b_scr = B("scr_row")
    b_scr2 = B("scr_row2")
    b_lgs = B("lgs")
    b_qpre = B("qpre")
    b_sqb = B("sqb")
    b_cf = B("cf_t")
    b_qrope = B("q_rope")
    b_wi = [B("wi0"), B("wi1")]
    b_cos = B("cosS")
    b_sin = B("sinS")
    b_pt = B("ptil")
    b_yb = B("yb")
    b_ctxf = B("ctxf")
    b_xn3 = B("xn3")
    b_wo = [B("wo0"), B("wo1")]
    b_stg = [B("stg0"), B("stg1")]
    b_p0a = [B("p0a0"), B("p0a1")]
    b_p0b = [B("p0b0"), B("p0b1")]
    b_pb = [B(f"pb{i}") for i in range(8)]
    b_small = B("small")
    b_consts = B("consts")
    b_bias = B("bias")
    b_wc = B("wcache")
    b_qm = B("qm")
    b_km = B("km")
    b_sh = B("sh")
    b_recl = B("recl")
    wcf = [0]

    def ph0():
        if "consts" not in PH0_PARTS:
            return
        kb.dma("sync", idf[:], c_idf[:, :], writes=[b_consts])
        kb.dma("sync", idb[:], c_idb[:, :], writes=[b_consts])
        kb.dma("sync", pswap[:], c_pswap[:, :], writes=[b_consts])
        kb.dma("sync", onecol[:], c_ones_col[:, :], writes=[b_consts])
        kb.dma("sync", onerow[:], c_ones_row[:, :], writes=[b_consts])
        kb.dma("sync", hd2_t[:], c_hd2[:, :], writes=[b_consts])
        kb.dma("sync", bas2_t[:], c_basis2[:, :], writes=[b_consts])
        kb.op("pool", lambda e: e.memset(epsb[:], EPS), writes=[b_consts])
        if "vaug" not in PH0_PARTS:
            return
        for h in range(H):
            kb.op("pool", lambda e, h=h: e.memset(vaug[:, :, h * 65 + 64:h * 65 + 65], 1.0),
                  writes=[b_vaug])

        if "bias" not in PH0_PARTS:
            return
        for src, dst, ncol, bb in ((bi_d, bias_in, 24, b_bias), (bo_d, bias_out, C, b_bias),
                                   (ga_d, gam_t, C, b_bias), (be_d, bet_t, C, b_bias)):
            for c in range(ncol):
                kb.dma("sync", rowtmp[0:1, 0:128], src[c * 128:(c + 1) * 128].unsqueeze(0),
                       writes=[b_small])
                kb.op("pe", lambda e: e.transpose(pb[0][:, 0:1], rowtmp[0:1, 0:128],
                                                  idf[0:1, 0:1]),
                      reads=[b_small, b_consts], writes=[b_pb[0]])
                kb.op("dve", lambda e, c=c, dst=dst: e.tensor_copy(dst[:, c:c + 1],
                                                                   pb[0][:, 0:1]),
                      reads=[b_pb[0]], writes=[bb])

        if "rw" not in PH0_PARTS:
            return
        kb.dma("pool", rw_sb[:], rw_d.rearrange("(c p) f -> p c f", p=128),
               writes=[b_consts])

    def wctap():
        kb.dma("sync", a32[:, 0:4, 0:512],
               wc_g[1].rearrange("(kc p) f -> p kc f", p=128)[:, 0:4, :],
               reads=[b_wc], writes=[b_a32])
        kb.dma("sync", a32[:, 4:8, 0:1024],
               wc_sg.rearrange("(kc p) f -> p kc f", p=128)[:, 0:4, :],
               reads=[b_wc], writes=[b_a32])
        for c in range(C):
            kb.op("dve", lambda e, c=c: e.tensor_copy(r_t[:, c, :], a32[:, c, :]),
                  reads=[b_a32], writes=[b_r])
        kb.op("dve", lambda e: e.tensor_copy(r_t[0:32, 0, 0:2048], sel_t[:, 0:2048]),
              reads=[b_consts], writes=[b_r])
        kb.barrier()

    def load(s):
        for tt in range(NKT):
            bi = tt % 2
            kb.dma("sync", stg[bi][:], Xd[s, tt * 128:(tt + 1) * 128, :], writes=[b_stg[bi]])
            for c in range(C):
                j = (tt * C + c) % 6
                kb.op("pe", lambda e, bi=bi, c=c, j=j: e.transpose(
                    pb[j][:, 0:128], stg[bi][:, c * 128:(c + 1) * 128], idf[:]),
                    reads=[b_stg[bi], b_consts], writes=[b_pb[j]])
                kb.op("dve", lambda e, c=c, tt=tt, j=j: e.tensor_copy(
                    r_t[:, c, tt * 128:(tt + 1) * 128], pb[j][:, 0:128]),
                    reads=[b_pb[j]], writes=[b_r])

    def ln_stats(g, get_bf, pbs_i, pbq_i, pba_i, pbb_i):
        for c in range(C):
            rd, ap = get_bf(c)
            qi = c % 2
            kb.op("act", lambda e, ap=ap, qi=qi: e.activation(qb2[qi][:], ap, AF.Square),
                  reads=rd, writes=[b_qb[qi]])
            kb.op("pe", lambda e, ap=ap, c=c: e.matmul(
                pb[pbs_i][0:1, :], onecol[:], ap, start=(c == 0), stop=(c == C - 1)),
                reads=rd + [b_consts], writes=[b_pb[pbs_i]])
            kb.op("pe", lambda e, qi=qi, c=c: e.matmul(
                pb[pbq_i][0:1, :], onecol[:], qb2[qi][:], start=(c == 0), stop=(c == C - 1)),
                reads=[b_qb[qi], b_consts], writes=[b_pb[pbq_i]])
        kb.op("dve", lambda e: e.tensor_single_scalar(nmu_row[:], pb[pbs_i][0:1, :],
                                                      -1.0 / D, ALU.mult),
              reads=[b_pb[pbs_i]], writes=[b_nmu])
        kb.op("dve", lambda e: e.tensor_single_scalar(scr_row[:], pb[pbq_i][0:1, :],
                                                      1.0 / D, ALU.mult),
              reads=[b_pb[pbq_i]], writes=[b_scr])
        kb.op("dve", lambda e: e.tensor_mul(rst_row[:], nmu_row[:], nmu_row[:]),
              reads=[b_nmu], writes=[b_rst])
        kb.op("dve", lambda e: e.tensor_sub(scr_row[:], scr_row[:], rst_row[:]),
              reads=[b_scr, b_rst], writes=[b_scr])
        kb.op("act", lambda e: e.activation(rst_row[:], scr_row[:], AF.Sqrt,
                                            bias=epsb[0:1, :]),
              reads=[b_scr, b_consts], writes=[b_rst])
        kb.op("dve", lambda e: e.reciprocal(rst_row[:], rst_row[:]),
              reads=[b_rst], writes=[b_rst])
        kb.op("pe", lambda e: e.matmul(pb[pba_i], onerow[:], nmu_row[0:1, :],
                                       start=True, stop=True),
              reads=[b_nmu, b_consts], writes=[b_pb[pba_i]])
        kb.op("pe", lambda e: e.matmul(pb[pbb_i], onerow[:], rst_row[0:1, :],
                                       start=True, stop=True),
              reads=[b_rst, b_consts], writes=[b_pb[pbb_i]])

    def r_chunk_bf(g):
        def get(c):
            bi = c % 2
            kb.op("act", lambda e, c=c, bi=bi: e.activation(
                rb2[bi][:], r_t[:, c, g * G:(g + 1) * G], AF.Copy),
                reads=[b_r], writes=[b_rb[bi]])
            return [b_rb[bi]], rb2[bi][:]
        return get

    def ffn(s, upto="full"):
        ENTRY = {n: kb.cnt[n] for n in ("pe", "act", "dve", "pool")}
        kb.barrier()
        _rs = nc.named_scope("lnrt")
        _rs.__enter__()
        for g in range(NGRP):
            ln_stats(g, r_chunk_bf(g), 0, 1, 6, 7)
            for c in range(C):
                kb.op("dve", lambda e, c=c, g=g: e.tensor_add(
                    tln[:], r_t[:, c, g * G:(g + 1) * G], pb[6]),
                    reads=[b_r, b_pb[6]], writes=[b_tln])
                kb.op("dve", lambda e, c=c, g=g: e.tensor_mul(
                    a32[:, c, g * G:(g + 1) * G], tln[:], pb[7]),
                    reads=[b_tln, b_pb[7]], writes=[b_a32])
        if upto == "ln":
            for c in range(C):
                for g in range(NGRP):
                    kb.op("dve", lambda e, c=c, g=g: e.tensor_copy(
                        r_t[:, c, g * G:(g + 1) * G], a32[:, c, g * G:(g + 1) * G]),
                        reads=[b_a32], writes=[b_r])
            kb.barrier()
            return
        kb.op("pool", lambda e: e.memset(coeff[:], 1.0), writes=[b_coeff])
        for g in range(NGRP):
            def logits(e, g=g):
                for c in range(C):
                    ins = e.matmul(pb[0][0:16, :], rw_sb[:, c, :],
                                   a32[:, c, g * G:(g + 1) * G],
                                   start=(c == 0), stop=(c == C - 1))
                return ins
            kb.op("pe", logits, reads=[b_a32, b_consts], writes=[b_pb[0]])
            kb.op("dve", lambda e: e.tensor_copy(lgs[:], pb[0][0:16, :]),
                  reads=[b_pb[0]], writes=[b_lgs])
            for tt in range(4):
                col = g * G + tt * 128
                kb.op("pe", lambda e, tt=tt: e.transpose(
                    pb[1][:, 0:16], lgs[0:16, tt * 128:(tt + 1) * 128], idf[0:16, 0:16]),
                    reads=[b_lgs, b_consts], writes=[b_pb[1]])
                kb.op("dve", lambda e: e.tensor_copy(lt_t[:], pb[1][:, 0:16]),
                      reads=[b_pb[1]], writes=[b_small])
                kb.op("dve", lambda e: e.reduce_max(nm0[:], lt_t[:], axis=AX.X, negate=True),
                      reads=[b_small], writes=[b_small])
                kb.op("act", lambda e: e.activation(ex_t[:], lt_t[:], AF.Exp, bias=nm0[:, :]),
                      reads=[b_small], writes=[b_small])
                kb.op("dve", lambda e: e.reduce_sum(ssum[:], ex_t[:], axis=AX.X),
                      reads=[b_small], writes=[b_small])
                kb.op("dve", lambda e: e.reciprocal(rs_t[:], ssum[:]),
                      reads=[b_small], writes=[b_small])
                kb.op("dve", lambda e: e.reduce_max(m1_t[:], ex_t[:], axis=AX.X),
                      reads=[b_small], writes=[b_small])
                kb.op("dve", lambda e: e.tensor_single_scalar(eq_t[:], ex_t[:], m1_t[:, :],
                                                              ALU.is_equal),
                      reads=[b_small], writes=[b_small])
                kb.op("dve", lambda e: e.scalar_tensor_tensor(eq_t[:], eq_t[:], -1e30,
                                                              ex_t[:], ALU.mult, ALU.add),
                      reads=[b_small], writes=[b_small])
                kb.op("dve", lambda e: e.reduce_max(m2_t[:], eq_t[:], axis=AX.X),
                      reads=[b_small], writes=[b_small])
                kb.op("dve", lambda e: e.tensor_single_scalar(sel_r[:], ex_t[:], m2_t[:, :],
                                                              ALU.is_ge),
                      reads=[b_small], writes=[b_small])
                kb.op("dve", lambda e: e.tensor_mul(ex_t[:], ex_t[:], sel_r[:]),
                      reads=[b_small], writes=[b_small])
                kb.op("dve", lambda e: e.tensor_single_scalar(ex_t[:], ex_t[:], rs_t[:, :],
                                                              ALU.mult),
                      reads=[b_small], writes=[b_small])
                kb.op("pe", lambda e: e.transpose(pb[2][0:16, 0:128], ex_t[:], idf[:]),
                      reads=[b_small, b_consts], writes=[b_pb[2]])
                kb.op("dve", lambda e, col=col: e.tensor_copy(
                    coeff[0:16, col:col + 128], pb[2][0:16, 0:128]),
                    reads=[b_pb[2]], writes=[b_coeff])
        if upto == "router":
            kb.op("dve", lambda e: e.tensor_copy(r_t[0:32, 0, :], coeff[:, :]),
                  reads=[b_coeff], writes=[b_r])
            kb.barrier()
            _rs.__exit__(None, None, None)
            return
        kb.barrier()
        _rs.__exit__(None, None, None)

        _ls = nc.named_scope("exploop")
        _ls.__enter__()
        ffn_loop_body(*ffn_wload(ENTRY))
        _ls.__exit__(None, None, None)
        if upto == "sgil":
            kb.op("dve", lambda e: e.tensor_copy(r_t[:, 0, 0:G], sgil[:, :]),
                  reads=[b_sgil], writes=[b_r])
            kb.op("dve", lambda e: e.tensor_copy(r_t[:, 4, 0:G], cb_s[:]),
                  reads=[b_cbs], writes=[b_r])
            kb.barrier()
            return
        if upto == "inter":
            for ei in range(2):
                for fc in range(FCH):
                    kb.op("dve", lambda e, ei=ei, fc=fc: e.tensor_copy(
                        r_t[:, ei * 4 + fc, 0:G], inter[:, ei, fc, :]),
                        reads=[b_inter], writes=[b_r])
            kb.barrier()


    def ffn_wload(ENTRY):
        P0 = kb.cnt["pe"]
        A0 = kb.cnt["act"]
        D0 = kb.cnt["dve"]
        WSL = (kb.cnt["wA"], kb.cnt["wB"], kb.cnt["wC"])
        WNM = ("wA", "wB", "wC")
        sems = kb.sem
        pe, act, dve, pool, sync = (kb.e["pe"], kb.e["act"], kb.e["dve"],
                                    kb.e["pool"], kb.e["sync"])

        def gu_src(e):
            if e < 16:
                return (gw_d[e].rearrange("(kc p) f -> p kc f", p=128),
                        uw_d[e].rearrange("(kc p) f -> p kc f", p=128))
            h0 = (e - 16) * 512
            return (sg_d[:, h0:h0 + 512].rearrange("(kc p) f -> p kc f", p=128),
                    su_d[:, h0:h0 + 512].rearrange("(kc p) f -> p kc f", p=128))

        def d_src(e):
            if e < 16:
                return dw_d[e].rearrange("(kc p) f -> p kc f", p=128)
            h0 = (e - 16) * 512
            return sd_d[h0:h0 + 512, :].rearrange("(kc p) f -> p kc f", p=128)

        NP = NE // 2
        for _sn in ("g0", "g1"):
            if kb.cnt[_sn] > 0 and kb.hwm["pool"][_sn] < kb.cnt[_sn]:
                pool.wait_ge(sems[_sn], kb.cnt[_sn])
                kb.hwm["pool"][_sn] = kb.cnt[_sn]
        for e_n, e_v in ENTRY.items():
            if e_n != "pool" and e_v > 0:
                pool.wait_ge(sems[e_n], e_v)
        for j in range(3 * NP):
            p3, r3 = divmod(j, 3)
            sl = slot[r3]
            if p3 >= 1:
                pm = p3 - 1
                if r3 == 0:
                    pool.wait_ge(sems["pe"], P0 + pm * 104 + 88)
                elif r3 == 1:
                    pool.wait_ge(sems["pe"], P0 + pm * 104 + 96)
                else:
                    pool.wait_ge(sems["pe"], P0 + pm * 104 + 104)
            wsem = WNM[r3]
            if r3 < 2:
                e_g = 2 * p3 + r3
                sga, sua = gu_src(e_g)
                pool.dma_start(out=sl[:, 0:4096].rearrange("p (kc f) -> p kc f", kc=8),
                               in_=sga).then_inc(sems[wsem], 16)
                pool.dma_start(out=sl[:, 4096:8192].rearrange("p (kc f) -> p kc f", kc=8),
                               in_=sua).then_inc(sems[wsem], 16)
                if r3 == 0:
                    pool.dma_start(out=sel_fx[:, :],
                                   in_=c_sel[:, 2 * p3 * 128:(2 * p3 + 2) * 128]
                                   ).then_inc(sems[wsem], 16)
                    kb.cnt[wsem] += 16
            else:
                pool.dma_start(out=sl[:, 0:4096].rearrange("p (kc f) -> p kc f", kc=4),
                               in_=d_src(2 * p3)).then_inc(sems[wsem], 16)
                pool.dma_start(out=sl[:, 4096:8192].rearrange("p (kc f) -> p kc f", kc=4),
                               in_=d_src(2 * p3 + 1)).then_inc(sems[wsem], 16)
            kb.cnt[wsem] += 32

        return P0, A0, D0, WSL, WNM

    def ffn_loop_body(P0, A0, D0, WSL, WNM):
        sems = kb.sem
        pe, act, dve = kb.e["pe"], kb.e["act"], kb.e["dve"]
        NP = NE // 2
        for p in range(NP):
            for g in range(NGRP):
                pg = p * 4 + g
                base = P0 + p * 104 + g * 26
                Dg = D0 + pg * 26
                for ei in range(2):
                    mm = pg * 2 + ei
                    st = base + ei * 9
                    sl = slot[ei]
                    if ei == 1:
                        pe.wait_ge(sems["dve"], Dg + 1)
                    elif pg >= 1:
                        pe.wait_ge(sems["dve"], Dg - 26 + 18 + 4 + 1)
                    if g == 0:
                        mult = 48 if ei == 0 else 32
                        pe.wait_ge(sems[WNM[ei]], WSL[ei] + mult * (p + 1))
                        if ei == 1:
                            pe.wait_ge(sems[WNM[0]], WSL[0] + 48 * (p + 1))
                    pe.matmul(pb[4], sel_fx[:, ei * 128:(ei + 1) * 128],
                              coeff[:, g * G:(g + 1) * G], start=True, stop=True
                              ).then_inc(sems["pe"], 1)
                    for fc in range(FCH):
                        if ei == 1:
                            pe.wait_ge(sems["dve"], Dg + 3 + 2 * fc)
                        elif pg >= 1:
                            pe.wait_ge(sems["dve"], Dg - 26 + 18 + fc + 1)
                        for kc in range(C):
                            ins = pe.matmul(
                                pb[fc],
                                sl[:, kc * 512 + fc * 128:kc * 512 + (fc + 1) * 128],
                                a32[:, kc, g * G:(g + 1) * G],
                                start=(kc == 0), stop=(kc == C - 1))
                        ins.then_inc(sems["pe"], 1)
                    for fc in range(FCH):
                        pe.wait_ge(sems["act"], A0 + mm * 4 + fc + 1)
                        for kc in range(C):
                            ins = pe.matmul(
                                pb[fc],
                                sl[:, 4096 + kc * 512 + fc * 128:
                                   4096 + kc * 512 + (fc + 1) * 128],
                                a32[:, kc, g * G:(g + 1) * G],
                                start=(kc == 0), stop=(kc == C - 1))
                        ins.then_inc(sems["pe"], 1)
                pe.wait_ge(sems["dve"], Dg + 18)
                if g == 0:
                    pe.wait_ge(sems[WNM[2]], WSL[2] + 32 * (p + 1))
                dsl = slot[2]
                for fo in range(8):
                    if fo >= 5 and pg >= 1:
                        pe.wait_ge(sems["dve"], Dg - 26 + 18 + fo + 1)
                    first = True
                    for ei in range(2):
                        for kc in range(FCH):
                            ins = pe.matmul(
                                pb[fo],
                                dsl[:, ei * 4096 + kc * 1024 + fo * 128:
                                    ei * 4096 + kc * 1024 + (fo + 1) * 128],
                                inter[:, ei, kc, :],
                                start=first, stop=(ei == 1 and kc == FCH - 1))
                            first = False
                    ins.then_inc(sems["pe"], 1)
        kb.cnt["pe"] += NP * 104

        for mm in range(8 * NP):
            pg, ei = divmod(mm, 2)
            p, g = divmod(pg, 4)
            st = P0 + p * 104 + g * 26 + ei * 9
            for fc in range(FCH):
                act.wait_ge(sems["pe"], st + 2 + fc)
                if fc >= 1:
                    act.wait_ge(sems["dve"], D0 + pg * 26 + ei * 9 + 1 + 2 * fc)
                elif mm >= 1:
                    pgm, eim = divmod(mm - 1, 2)
                    act.wait_ge(sems["dve"], D0 + pgm * 26 + eim * 9 + 9)
                act.activation(sgil[:, :], pb[fc], AF.Silu).then_inc(sems["act"], 1)
        kb.cnt["act"] += 8 * NP * 4

        dcnt = D0
        for p in range(NP):
            for g in range(NGRP):
                pg = p * 4 + g
                for ei in range(2):
                    mm = pg * 2 + ei
                    st = P0 + p * 104 + g * 26 + ei * 9
                    if ei == 0:
                        dve.wait_ge(sems["pe"], P0 + p * 104 + g * 26)
                    dve.wait_ge(sems["pe"], st + 1)
                    if mm >= 1:
                        dve.wait_ge(sems["dve"], dcnt)
                    dve.tensor_copy(cb_s[:], pb[4]).then_inc(sems["dve"], 1)
                    dcnt += 1
                    for fc in range(FCH):
                        dve.wait_ge(sems["act"], A0 + mm * 4 + fc + 1)
                        dve.wait_ge(sems["dve"], dcnt)
                        dve.tensor_mul(sgil[:, :], sgil[:, :],
                                       cb_s[:]).then_inc(sems["dve"], 1)
                        dcnt += 1
                        dve.wait_ge(sems["pe"], st + 6 + fc)
                        dve.wait_ge(sems["dve"], dcnt)
                        dve.tensor_mul(inter[:, ei, fc, :],
                                       sgil[:, :], pb[fc]).then_inc(sems["dve"], 1)
                        dcnt += 1
                for fo in range(8):
                    dve.wait_ge(sems["pe"], P0 + p * 104 + g * 26 + 18 + fo + 1)
                    dve.scalar_tensor_tensor(r_t[:, fo, g * G:(g + 1) * G],
                                             pb[fo], 0.5,
                                             r_t[:, fo, g * G:(g + 1) * G],
                                             ALU.mult, ALU.add).then_inc(sems["dve"], 1)
                    dcnt += 1
        kb.cnt["dve"] += NP * 104
        kb.barrier()
    def attn(s):
        kb.barrier()
        kb.sync_join()
        kb.op("pool", lambda e: e.memset(qm2[:], 0.0), writes=[b_qm])
        kb.op("pool", lambda e: e.memset(km2[:], 0.0), writes=[b_km])
        kb.dma("sync", cosS[:], c_cos[:, :], writes=[b_cos])
        kb.dma("sync", sinS[:], c_sin[:, :], writes=[b_sin])

        def yb_chunk(c):
            return [b_yb], yb[:, c, :]

        _qs = nc.named_scope("qkv")
        _qs.__enter__()
        for g in range(NGRP):
            gsl = slice(g * G, (g + 1) * G)
            ln_stats(g, r_chunk_bf(g), 0, 1, 6, 7)
            for c in range(C):
                kb.op("dve", lambda e, c=c, gsl=gsl: e.tensor_add(
                    tln[:], r_t[:, c, gsl], pb[6]),
                    reads=[b_r, b_pb[6]], writes=[b_tln])
                kb.op("dve", lambda e, c=c: e.tensor_mul(yb[:, c, :], tln[:], pb[7]),
                      reads=[b_tln, b_pb[7]], writes=[b_yb])
            ln_stats(g, yb_chunk, 0, 1, 4, 5)
            for c in range(C):
                kb.op("dve", lambda e, c=c: e.tensor_add(tln[:], yb[:, c, :], pb[4]),
                      reads=[b_yb, b_pb[4]], writes=[b_tln])
                kb.op("dve", lambda e, c=c: e.tensor_mul(ys_q[:], tln[:], pb[5][:, :]),
                      reads=[b_tln, b_pb[5]], writes=[b_ys])
                kb.op("dve", lambda e, c=c: e.tensor_scalar(
                    xn3[:, c, :], ys_q[:], gam_t[:, c:c + 1], bet_t[:, c:c + 1],
                    ALU.mult, ALU.add),
                    reads=[b_ys, b_bias], writes=[b_xn3])
            for fo in range(24):
                wbuf = fo % 2
                kb.dma("pool", wi_b[wbuf][:],
                       Wi_d[:, fo * 128:(fo + 1) * 128].rearrange("(c p) f -> p c f", p=128),
                       writes=[b_wi[wbuf]])
                pj = fo % 3
                def qkvmm(e, wbuf=wbuf, pj=pj):
                    for c in range(C):
                        ins = e.matmul(pb[pj], wi_b[wbuf][:, c, :], xn3[:, c, :],
                                       start=(c == 0), stop=(c == C - 1))
                    return ins
                kb.op("pe", qkvmm, reads=[b_wi[wbuf], b_xn3], writes=[b_pb[pj]])
                if fo < 16:
                    isq = fo < 8
                    dst = q_rope if isq else a32
                    dstb = b_qrope if isq else b_a32
                    cc = fo if isq else fo - 8
                    kb.op("dve", lambda e, pj=pj, fo=fo: e.tensor_single_scalar(
                        qpre[:], pb[pj], bias_in[:, fo:fo + 1], ALU.add),
                        reads=[b_pb[pj], b_bias], writes=[b_qpre])
                    kb.op("pe", lambda e: e.matmul(pb[3][:, :], pswap[:], qpre[:],
                                                   start=True, stop=True),
                          reads=[b_qpre, b_consts], writes=[b_pb[3]])
                    kb.op("dve", lambda e, gsl=gsl: e.tensor_mul(tln[:], qpre[:],
                                                                 cosS[:, gsl]),
                          reads=[b_qpre, b_cos], writes=[b_tln])
                    kb.op("dve", lambda e, gsl=gsl: e.tensor_mul(ys_q[:], pb[3][:, :],
                                                                 sinS[:, gsl]),
                          reads=[b_pb[3], b_sin], writes=[b_ys])
                    kb.op("dve", lambda e, dst=dst, cc=cc, gsl=gsl: e.tensor_add(
                        dst[:, cc, gsl], tln[:], ys_q[:]),
                        reads=[b_tln, b_ys], writes=[dstb])
                    kb.op("act", lambda e, dst=dst, cc=cc, gsl=gsl: e.activation(
                        sqb[:], dst[:, cc, gsl], AF.Square),
                        reads=[dstb], writes=[b_sqb])
                    kb.op("pe", lambda e: e.matmul(pb[4][0:2, :], hd2_t[:], sqb[:],
                                                   start=True, stop=True),
                          reads=[b_sqb, b_consts], writes=[b_pb[4]])
                    kb.op("dve", lambda e: e.reduce_max(rm2[:], pb[4][0:2, :], axis=AX.X),
                          reads=[b_pb[4]], writes=[b_small])
                    mt = qm2 if isq else km2
                    mb = b_qm if isq else b_km
                    kb.op("dve", lambda e, mt=mt, cc=cc: e.tensor_max(
                        mt[0:2, cc:cc + 1], mt[0:2, cc:cc + 1], rm2[:]),
                        reads=[b_small], writes=[mb])
                else:
                    vc = fo - 16
                    kb.op("dve", lambda e, pj=pj, fo=fo: e.tensor_single_scalar(
                        qpre[:], pb[pj], bias_in[:, fo:fo + 1], ALU.add),
                        reads=[b_pb[pj], b_bias], writes=[b_qpre])
                    for t4 in range(4):
                        kb.op("pe", lambda e, t4=t4: e.transpose(
                            pb[5].bitcast(BF16)[:, 0:128],
                            qpre[:, t4 * 128:(t4 + 1) * 128], idb[:]),
                            reads=[b_qpre, b_consts], writes=[b_pb[5]])
                        kt = g * 4 + t4
                        hA = 2 * vc
                        kb.op("dve", lambda e, kt=kt, hA=hA: e.tensor_copy(
                            vaug[:, kt, hA * 65:hA * 65 + 64],
                            pb[5].bitcast(BF16)[:, 0:64]),
                            reads=[b_pb[5]], writes=[b_vaug])
                        kb.op("dve", lambda e, kt=kt, hA=hA: e.tensor_copy(
                            vaug[:, kt, (hA + 1) * 65:(hA + 1) * 65 + 64],
                            pb[5].bitcast(BF16)[:, 64:128]),
                            reads=[b_pb[5]], writes=[b_vaug])

        kb.op("dve", lambda e: e.tensor_mul(nsh[:], qm2[:], km2[:]),
              reads=[b_qm, b_km], writes=[b_sh])
        kb.op("act", lambda e: e.activation(nsh2[:], nsh[:], AF.Sqrt),
              reads=[b_sh], writes=[b_sh])
        kb.op("dve", lambda e: e.tensor_single_scalar(nsh[:], nsh2[:], -0.125, ALU.mult),
              reads=[b_sh], writes=[b_sh])
        for rr in range(2):
            kb.op("pe", lambda e, rr=rr: e.matmul(
                pb[6][:, rr * 8:(rr + 1) * 8], bas2_t[:, rr * 128:(rr + 1) * 128],
                nsh[0:2, 0:8], start=True, stop=True),
                reads=[b_sh, b_consts], writes=[b_pb[6]])
        kb.op("dve", lambda e: e.tensor_copy(shbc[:], pb[6][:, 0:16]),
              reads=[b_pb[6]], writes=[b_sh])
        _qs.__exit__(None, None, None)
        kb.barrier()
        kb.sync_join()
        _as = nc.named_scope("aloop")
        _as.__enter__()

        for qg in range(NGRP):
            qsl = slice(qg * G, (qg + 1) * G)
            for h in range(H):
                c2 = h // 2
                ro = (h % 2) * 64
                jcol = (h % 2) * 8 + c2
                pc = 4 + (h % 2)
                for kt in range(NKT):
                    pj = kt % 3
                    kb.op("pe", lambda e, pj=pj, c2=c2, ro=ro, kt=kt, qsl=qsl: e.matmul(
                        pb[pj],
                        a32[ro:ro + 64, c2, kt * 128:(kt + 1) * 128],
                        q_rope[ro:ro + 64, c2, qsl], start=True, stop=True),
                        reads=[b_a32, b_qrope], writes=[b_pb[pj]])
                    kb.op("act", lambda e, pj=pj, kt=kt, jcol=jcol: e.activation(
                        ptil[:, kt, :], pb[pj], AF.Exp,
                        bias=shbc[:, jcol:jcol + 1], scale=0.125),
                        reads=[b_pb[pj], b_sh], writes=[b_pt])
                    if kt in (5, 11):
                        fb = 3 if kt == 5 else 6
                        kb.op("pe", lambda e, fb=fb, qsl=qsl: e.matmul(
                            pb[fb], idb[:], a32[:, 0, qsl], start=True, stop=True),
                            reads=[b_a32, b_consts], writes=[b_pb[fb]])
                def ctxmm(e, pc=pc, h=h):
                    for kt in range(NKT):
                        ins = e.matmul(pb[pc][0:65, :], vaug[:, kt, h * 65:h * 65 + 65],
                                       ptil[:, kt, :], start=(kt == 0), stop=(kt == NKT - 1))
                    return ins
                kb.op("pe", ctxmm, reads=[b_pt, b_vaug], writes=[b_pb[pc]])
                kb.op("dve", lambda e, pc=pc: e.reciprocal(recl_row[:], pb[pc][64:65, :]),
                      reads=[b_pb[pc]], writes=[b_recl])
                kb.op("pe", lambda e: e.matmul(pb[7][0:64, :], onerow[0:1, 0:64],
                                               recl_row[0:1, :], start=True, stop=True),
                      reads=[b_recl, b_consts], writes=[b_pb[7]])
                kb.op("act", lambda e, pc=pc: e.activation(cf_t[0:64, :], pb[pc][0:64, :],
                                                           AF.Copy),
                      reads=[b_pb[pc]], writes=[b_cf])
                kb.op("dve", lambda e, c2=c2, ro=ro: e.tensor_mul(
                    ctxf[ro:ro + 64, c2, :], cf_t[0:64, :], pb[7][0:64, :]),
                    reads=[b_cf, b_pb[7]], writes=[b_ctxf])
            ln_stats(qg, r_chunk_bf(qg), 0, 1, 2, 3)
            for fo in range(C):
                wbuf = fo % 2
                kb.dma("pool", wo_b[wbuf][:],
                       Wo_d[:, fo * 128:(fo + 1) * 128].rearrange("(c p) f -> p c f", p=128),
                       writes=[b_wo[wbuf]])
                pj = fo % 2 + 6
                def projmm(e, wbuf=wbuf, pj=pj):
                    for c in range(C):
                        ins = e.matmul(pb[pj], wo_b[wbuf][:, c, :], ctxf[:, c, :],
                                       start=(c == 0), stop=(c == C - 1))
                    return ins
                kb.op("pe", projmm, reads=[b_wo[wbuf], b_ctxf], writes=[b_pb[pj]])
                kb.op("dve", lambda e, fo=fo, qsl=qsl: e.tensor_add(
                    tln[:], r_t[:, fo, qsl], pb[2][:, :]),
                    reads=[b_r, b_pb[2]], writes=[b_tln])
                kb.op("dve", lambda e: e.tensor_mul(ys_l[:], tln[:], pb[3][:, :]),
                      reads=[b_tln, b_pb[3]], writes=[b_cf])
                kb.op("dve", lambda e, pj=pj, fo=fo: e.scalar_tensor_tensor(
                    tln[:], pb[pj], bias_out[:, fo:fo + 1], ys_l[:], ALU.add, ALU.add),
                    reads=[b_pb[pj], b_cf, b_bias], writes=[b_tln])
                kb.op("dve", lambda e, fo=fo, qsl=qsl: e.tensor_add(
                    r_t[:, fo, qsl], r_t[:, fo, qsl], tln[:]),
                    reads=[b_tln], writes=[b_r])
        _as.__exit__(None, None, None)
        kb.barrier()

    def store(s):
        for tt in range(NKT):
            bi = tt % 2
            for c in range(C):
                j = (tt * C + c) % 6
                kb.op("pe", lambda e, c=c, tt=tt, j=j: e.transpose(
                    pb[j][:, 0:128], r_t[:, c, tt * 128:(tt + 1) * 128], idf[:]),
                    reads=[b_r, b_consts], writes=[b_pb[j]])
                kb.op("dve", lambda e, c=c, bi=bi, j=j: e.tensor_copy(
                    stg[bi][:, c * 128:(c + 1) * 128], pb[j][:, 0:128]),
                    reads=[b_pb[j]], writes=[b_stg[bi]])
            kb.dma("sync", OUT[s, tt * 128:(tt + 1) * 128, :], stg[bi][:],
                   reads=[b_stg[bi]])

    from contextlib import nullcontext
    def scope(name):
        return nc.named_scope(name)
    with scope("ph0"):
        ph0()
    for s in range(nseq):
        kb.barrier()
        if stage == "wc":
            wctap()
            store(s)
            continue
        with scope(f"load{s}"):
            load(s)
        if stage != "load":
            with scope(f"ffnA{s}"):
                ffn(s, upto=(stage if stage in ("ln", "router", "inter", "sgil") else "full"))
        if stage in ("attn", "full"):
            with scope(f"attn{s}"):
                attn(s)
        if stage == "full":
            with scope(f"ffnB{s}"):
                ffn(s)
        kb.barrier()
        with scope(f"store{s}"):
            store(s)
    for _sn in ("d0", "d1", "d2", "d3", "d4", "d5", "g0", "g1",
                "wA", "wB", "wC"):
        if kb.cnt[_sn] > 0:
            kb.e["sync"].wait_ge(kb.sem[_sn], kb.cnt[_sn])
    return nc


WNAMES = ["L0_attn_W_in", "L0_attn_b_in", "L0_attn_W_out", "L0_attn_b_out",
          "L0_attn_gamma", "L0_attn_beta", "L0_ffn_router_w", "L0_ffn_gate_w",
          "L0_ffn_up_w", "L0_ffn_down_w", "L0_ffn_shared_gate_w",
          "L0_ffn_shared_up_w", "L0_ffn_shared_down_w"]


def kernel(**inputs):
    X = np.ascontiguousarray(np.asarray(inputs["L0_attn_X"], dtype=np.float32))
    weights = {n: np.ascontiguousarray(np.asarray(inputs[n], dtype=np.float32))
               for n in WNAMES}
    shards = np.split(X, 8, axis=0)
    nc = build_nc(nseq=4, stage="full")
    in_maps = [dict(weights, L0_attn_X=shards[i]) for i in range(8)]
    res = run_bass_kernel_spmd(nc, in_maps, core_ids=list(range(8)))
    return np.concatenate([r["OUT"] for r in res.results], axis=0)
